# version 27
# baseline (speedup 1.0000x reference)
import math
import numpy as np
import ml_dtypes
from contextlib import ExitStack

import concourse.bass as bass
import concourse.mybir as mybir
from concourse.bass_utils import run_bass_kernel_spmd

F32 = mybir.dt.float32
BF16 = mybir.dt.bfloat16
AF = mybir.ActivationFunctionType
ALU = mybir.AluOpType

D = 1024
T = 2048
DIN = 2048
NH = 32
HD = 64
DST = 128
CONVD = 2304
DIP = 4384
NCHUNK = 16
CQ = 128
NE = 8
FF = 2048
EPS = 1e-5
BIG = 1e9
TPAD = 2056

KB = 1024

DEBUG_TAPS = set()
BUILD_UPTO = 99
SKIP_ROUTER = False
MAIN_LIMIT = None
MAIN_DIRS = ("f", "r")

_cache = {}


class Bld:
    ENGS = ("tensor", "vector", "scalar", "gpsimd", "sync")

    def __init__(self, nc, es):
        self.nc = nc
        self.es = es
        self.ops = {e: [] for e in self.ENGS}
        self.sems = {}
        self.cnt = {}
        self.sb = {}
        self.dr = {}
        self.ps = None

    def sem(self, name):
        s = self.es.enter_context(self.nc.semaphore(name))
        self.sems[name] = s
        self.cnt[name] = 0
        return s

    def emit(self, eng, fn):
        self.ops[eng].append(fn)

    def inc(self, eng, tag, n, fn):
        self.cnt[tag] += n
        sem = self.sems[tag]

        def wrapped(e, fn=fn, sem=sem, n=n):
            fn(e).then_inc(sem, n)
        self.emit(eng, wrapped)
        return self.cnt[tag]

    def wait(self, eng, tag, at=None):
        v = self.cnt[tag] if at is None else at
        if v <= 0:
            return
        sem = self.sems[tag]
        self.emit(eng, lambda e, sem=sem, v=v: e.wait_ge(sem, v))

    def mark(self, tag):
        return self.cnt[tag]

    def barrier(self):
        marks = {t: self.cnt[t] for t in self.cnt}
        for eng in self.ENGS:
            for t, v in marks.items():
                if v > 0:
                    self.wait(eng, t, v)

    def tiny(self, eng, tag, fn):
        self.inc(eng, tag, 1, fn)
        self.wait(eng, tag)

    def bank(self, b, cols=512, off=0, dt=F32):
        ap = self.ps.ap()
        if dt == F32:
            return ap[:, b * 512 + off: b * 512 + off + cols]
        return ap.bitcast(BF16)[:, b * 1024 + off: b * 1024 + off + cols]


CARVE_OFFS = {}


def carve(nc, es, base, specs, limit=16384 + 201728 + 128):
    out = {}
    off = base
    for name, shape, dt in specs:
        esz = 4 if dt == F32 else (2 if dt == BF16 else 1)
        nbytes = int(np.prod(shape[1:])) * esz if len(shape) > 1 else esz
        off = (off + 31) // 32 * 32
        h = nc.alloc_sbuf_tensor_at(name, list(shape), dt, offset=off)
        out[name] = h
        CARVE_OFFS[name] = off
        off += nbytes
    assert off <= limit, f"carve overflow: end={off} > {limit}"
    return out, off


def bcast_ap(ap2d, reps, inner):
    return ap2d.unsqueeze(2).broadcast_to([ap2d.shape[0], ap2d.shape[1], reps])


def bcast_outer(ap2d, reps):
    return ap2d.unsqueeze(1).broadcast_to([ap2d.shape[0], reps, ap2d.shape[1]])


MB = 16896
MSPAN = 201728


def build_program():
    nc = bass.Bass("TRN2", target_bir_lowering=False)
    nc.alloc_sbuf_tensor("manual_guard", [128, MSPAN], mybir.dt.uint8)
    es = ExitStack()
    k = Bld(nc, es)

    DI = {}

    def din(name, shape, dt=F32):
        DI[name] = nc.dram_tensor(name, list(shape), dt, kind="ExternalInput").ap()
        return DI[name]

    din("xtok", [T, D]); din("xfm", [D, T])
    din("winT", [D, DIP]); din("woutT", [DIN, D])
    for d in ("f", "r"):
        din(f"convw_{d}", [CONVD, 4]); din(f"convb_{d}", [CONVD, 1])
        din(f"dtb_{d}", [NH, 1]); din(f"alog_{d}", [NH, 1])
        din(f"drow_{d}", [1, NH]); din(f"nwrow_{d}", [1, DIN])
    din("router", [D, NE])
    din("gEDF", [NE, D, FF]); din("uEDF", [NE, D, FF]); din("dEFD", [NE, FF, D])
    din("ulc", [128, 128]); din("urc", [128, 128])
    din("sel127c", [128, 128]); din("sel0c", [128, 128])
    din("idf32c", [128, 128]); din("idbfc", [128, 128])
    din("ones2c", [2, 128]); din("ohcc", [96, NH])
    din("nblfc", [128, 512]); din("nblrc", [128, 512])
    din("ones1fc", [1, 128]); din("sel8c", [NE, 1024])
    din("flagsc", [128, 8])

    def dscratch(name, shape, dt):
        kind = "ExternalOutput" if name in DEBUG_TAPS else "Internal"
        h = nc.dram_tensor(name, list(shape), dt, kind=kind)
        k.dr[name] = h
        return h

    dscratch("xstd_f", [128, NCHUNK * DIN], BF16)
    dscratch("xstd_r", [128, NCHUNK * DIN], BF16)
    dscratch("siluzd", [128, NCHUNK * DIN], BF16)
    dscratch("y3fd", [128, NCHUNK * DIN], BF16)
    dscratch("ysumd", [128, NCHUNK * DIN], BF16)
    dscratch("accd", [128, 8 * T], F32)
    dscratch("sndE", [128, 128], BF16)
    dscratch("rcvE", [128, 128], BF16)
    dscratch("sndS", [128, 2 * DIN], BF16)
    dscratch("rcvS", [128, 2 * DIN], BF16)
    ofm = nc.dram_tensor("ofm", [128, 8 * T], F32, kind="ExternalOutput")
    k.dr["ofm"] = ofm

    pa, pa_end = carve(nc, es, MB, [
        ("xnT", [128, 8 * T], BF16),
        ("fullw", [128, 128], F32),
        ("ulc", [128, 128], F32), ("urc", [128, 128], F32),
        ("sel127", [128, 128], F32), ("sel0", [128, 128], F32),
        ("idf32", [128, 128], F32), ("idbf", [128, 128], BF16),
        ("ones2", [2, 128], BF16), ("ohc", [96, NH], BF16),
        ("ones1f", [1, 128], F32), ("sel8", [NE, 1024], BF16),
        ("flags", [128, 8], F32),
        ("convw_f", [128, 72], F32), ("convw_r", [128, 72], F32),
        ("convb_f", [128, 18], F32), ("convb_r", [128, 18], F32),
        ("dtbcol_f", [NH, 1], F32), ("dtbcol_r", [NH, 1], F32),
        ("nacol_f", [NH, 1], F32), ("nacol_r", [NH, 1], F32),
        ("alogcol_f", [NH, 1], F32), ("alogcol_r", [NH, 1], F32),
    ])
    pb, pb_end = carve(nc, es, pa_end, [
        ("stack96_f", [96, NCHUNK * 128], BF16), ("stack96_r", [96, NCHUNK * 128], BF16),
        ("lbce_f", [128, NCHUNK * NH], BF16), ("lbce_r", [128, NCHUNK * NH], BF16),
        ("ddt_f", [128, NCHUNK * NH], BF16), ("ddt_r", [128, NCHUNK * NH], BF16),
        ("bcfm_f", [128, 2 * T], BF16), ("bcfm_r", [128, 2 * T], BF16),
        ("btok_f", [128, T], BF16), ("btok_r", [128, T], BF16),
        ("s0t", [128, DIN], BF16),
        ("sin_f", [128, DIN], BF16), ("sin_r", [128, DIN], BF16),
        ("nwbc_f", [128, DIN], BF16), ("nwbc_r", [128, DIN], BF16),
        ("dbc_f", [128, NH], F32), ("dbc_r", [128, NH], F32),
        ("rsb", [128, 8 * NE], F32),
        ("rhi", [128, NE * NE], BF16), ("rlo", [128, NE * NE], BF16),
    ])
    TB = pb_end
    k.sb.update(pa); k.sb.update(pb)
    k.TB = TB
    assert TB <= MB + 102 * KB, f"persist too large: {TB}"
    p0, _ = carve(nc, es, TB, [
        ("drow_f", [1, NH], F32), ("drow_r", [1, NH], F32),
        ("nwrow_f", [1, DIN], BF16), ("nwrow_r", [1, DIN], BF16),
    ])
    k.sb.update(p0)

    k.ps = es.enter_context(nc.psum_tensor("PS", [128, 4096], F32))

    for s in ("qgp", "qsy", "qsc", "pe", "dv", "ac", "pl", "cc"):
        k.sem(s)

    sb = k.sb

    def S(name):
        return k.sb[name].ap()

    k.S = S
    k.DI = DI

    phases_built = []

    def ph0():
        g = lambda fn: k.inc("gpsimd", "qgp", 16, fn)
        for nm in ("ulc", "urc", "sel127", "sel0", "idf32", "ones1f", "sel8",
                   "flags", "fullw"):
            pass
        g(lambda e: e.dma_start(out=S("ulc"), in_=DI["ulc"]))
        g(lambda e: e.dma_start(out=S("urc"), in_=DI["urc"]))
        g(lambda e: e.dma_start(out=S("sel127"), in_=DI["sel127c"]))
        g(lambda e: e.dma_start(out=S("sel0"), in_=DI["sel0c"]))
        g(lambda e: e.dma_start(out=S("idf32"), in_=DI["idf32c"]))
        g(lambda e: e.dma_start(out=S("ones1f"), in_=DI["ones1fc"]))
        g(lambda e: e.dma_start(out=S("sel8"), in_=DI["sel8c"]))
        g(lambda e: e.dma_start(out=S("flags"), in_=DI["flagsc"]))
        g(lambda e: e.dma_start(out=S("idbf"), in_=DI["idbfc"]))
        g(lambda e: e.dma_start(out=S("ones2"), in_=DI["ones2c"]))
        g(lambda e: e.dma_start(out=S("ohc"), in_=DI["ohcc"]))
        for d in ("f", "r"):
            g(lambda e, d=d: e.dma_start(
                out=S(f"convw_{d}").rearrange("p (j c) -> p j c", c=4),
                in_=DI[f"convw_{d}"].rearrange("(j p) c -> p j c", p=128)))
            g(lambda e, d=d: e.dma_start(
                out=S(f"convb_{d}").unsqueeze(2),
                in_=DI[f"convb_{d}"].rearrange("(j p) c -> p j c", p=128),
                allow_slow_non_contiguous=True))
            g(lambda e, d=d: e.dma_start(out=S(f"dtbcol_{d}"), in_=DI[f"dtb_{d}"]))
            g(lambda e, d=d: e.dma_start(out=S(f"alogcol_{d}"), in_=DI[f"alog_{d}"]))
            g(lambda e, d=d: e.dma_start(out=S(f"drow_{d}"), in_=DI[f"drow_{d}"]))
            g(lambda e, d=d: e.dma_start(out=S(f"nwrow_{d}"), in_=DI[f"nwrow_{d}"]))
        g(lambda e: e.dma_start(
            out=S("rsb").rearrange("p (kc e) -> p kc e", e=NE),
            in_=DI["router"].rearrange("(kc p) e -> p kc e", p=128)))
        ld0 = k.mark("qgp")

        k.wait("scalar", "qgp", ld0)
        k.inc("scalar", "ac", 1, lambda e: e.activation(
            S("nacol_f"), S("alogcol_f"), AF.Exp, scale=1.0))
        k.inc("scalar", "ac", 1, lambda e: e.activation(
            S("nacol_r"), S("alogcol_r"), AF.Exp, scale=1.0))
        k.wait("vector", "ac")
        k.inc("vector", "dv", 1, lambda e: e.tensor_scalar(
            S("nacol_f"), S("nacol_f"), -1.0, None, ALU.mult))
        k.inc("vector", "dv", 1, lambda e: e.tensor_scalar(
            S("nacol_r"), S("nacol_r"), -1.0, None, ALU.mult))
        k.wait("vector", "qgp", ld0)
        rsb3 = S("rsb").rearrange("p (kc e) -> p kc e", e=NE)
        rhi3 = S("rhi").rearrange("p (kc e) -> p kc e", e=NE)
        rlo3 = S("rlo").rearrange("p (kc e) -> p kc e", e=NE)
        k.inc("vector", "dv", 1, lambda e: e.tensor_copy(rhi3, rsb3))
        k.inc("vector", "dv", 1, lambda e: e.scalar_tensor_tensor(
            rlo3, rsb3, 1.0, rhi3, ALU.mult, ALU.subtract))
        k.rhi_dv = k.cnt["dv"]
        k.wait("tensor", "qgp", ld0)
        for d in ("f", "r"):
            for q in range(4):
                k.inc("tensor", "pe", 1, lambda e, d=d, q=q: e.matmul(
                    k.bank(q, 512), S("ones2")[0:1, :],
                    S(f"nwrow_{d}")[:, q * 512:(q + 1) * 512],
                    start=True, stop=True))
            k.wait("scalar", "pe")
            k.inc("scalar", "ac", 1, lambda e, d=d: e.activation(
                S(f"nwbc_{d}"), k.ps.ap()[:, 0:2048], AF.Copy))
            k.wait("tensor", "ac")
        for d in ("f", "r"):
            k.inc("tensor", "pe", 1, lambda e, d=d: e.matmul(
                k.bank(4, NH), S("ones1f"), S(f"drow_{d}"), start=True, stop=True))
            k.wait("vector", "pe")
            k.inc("vector", "dv", 1, lambda e, d=d: e.tensor_copy(
                S(f"dbc_{d}"), k.bank(4, NH)))
            k.wait("tensor", "dv")
        k.ld0 = ld0

    def tap(name, ap_fn, shape, dt, waits=()):
        if name not in DEBUG_TAPS:
            return
        h = nc.dram_tensor(name + "__dbg", list(shape), dt, kind="ExternalOutput")
        for tag in waits:
            k.wait("scalar", tag)
        k.inc("scalar", "qsc", 16, lambda e: e.dma_start(out=h.ap(), in_=ap_fn()))
    k.tap = tap

    bank_last = {}

    def bank_guard(b):
        if b in bank_last:
            tag, cnt = bank_last[b]
            k.wait("tensor", tag, cnt)

    def bank_consumed(b, tag):
        bank_last[b] = (tag, k.cnt[tag])
    k.bank_guard = bank_guard
    k.bank_consumed = bank_consumed

    k.c0ap = nc.const_aps.tensor(0.0, [128, 1], F32)
    k.c1ap = nc.const_aps.tensor(1.0, [128, 1], F32)
    k.tap0 = None
    if "constap" in DEBUG_TAPS:
        h0 = nc.dram_tensor("constap__dbg", [128, 2], F32, kind="ExternalOutput")
        k.tap0 = h0

    ph0()
    phases_built.append(0)

    p1, _ = carve(nc, es, TB, [
        ("xt0", [128, D], F32), ("xt1", [128, D], F32),
        ("xnf", [128, D], F32),
        ("xnb", [128, D], BF16), ("xlo", [128, D], BF16), ("junkb", [128, D], BF16),
        ("junkf", [128, D], F32),
        ("lncol", [128, 128], F32), ("lncolB", [128, 64], F32), ("rtB", [128, 4], F32),
        ("rt", [128, 12], F32),
        ("rte", [128, NE], F32), ("rtp", [128, NE], F32), ("rtq", [128, NE], F32),
        ("rteq1", [128, NE], F32), ("rteq2", [128, NE], F32),
        ("xnloT", [128, 8 * T], BF16),
    ])
    k.sb.update(p1)

    def ph1():
        k.barrier()
        xt = [S("xt0"), S("xt1")]
        xnT3 = S("xnT").rearrange("p (kc t) -> p kc t", kc=8)
        xloT3 = S("xnloT").rearrange("p (kc t) -> p kc t", kc=8)
        buf_free = [0, 0]
        for tc in range(NCHUNK):
            lnb = tc * 8
            bi = tc % 2
            k.wait("scalar", "dv", buf_free[bi])
            qm = k.inc("scalar", "qsc", 16, lambda e, tc=tc, bi=bi: e.dma_start(
                out=xt[bi], in_=DI["xtok"][tc * 128:(tc + 1) * 128, :]))
            k.wait("vector", "qsc", qm)
            k.tiny("vector", "dv", lambda e, bi=bi, lnb=lnb: e.reduce_sum(
                S("lncol")[:, lnb + 0:lnb + 1], xt[bi], axis=mybir.AxisListType.X))
            k.inc("vector", "dv", 1, lambda e, bi=bi: e.tensor_tensor(
                S("junkf"), xt[bi], xt[bi], ALU.mult))
            k.tiny("vector", "dv", lambda e, lnb=lnb: e.reduce_sum(
                S("lncol")[:, lnb + 2:lnb + 3], S("junkf"), axis=mybir.AxisListType.X))
            k.tiny("vector", "dv", lambda e, lnb=lnb: e.tensor_scalar(
                S("lncol")[:, lnb + 1:lnb + 2], S("lncol")[:, lnb + 0:lnb + 1], 1.0 / D, None, ALU.mult))
            k.tiny("vector", "dv", lambda e, lnb=lnb: e.tensor_scalar(
                S("lncol")[:, lnb + 3:lnb + 4], S("lncol")[:, lnb + 1:lnb + 2],
                S("lncol")[:, lnb + 1:lnb + 2], None, ALU.mult))
            k.tiny("vector", "dv", lambda e, lnb=lnb: e.scalar_tensor_tensor(
                S("lncol")[:, lnb + 4:lnb + 5], S("lncol")[:, lnb + 2:lnb + 3], 1.0 / D, S("lncol")[:, lnb + 3:lnb + 4],
                ALU.mult, ALU.subtract))
            k.wait("scalar", "dv")
            lbB = tc * 4
            k.tiny("vector", "dv", lambda e, lnb=lnb: e.tensor_scalar(
                S("lncol")[:, lnb + 5:lnb + 6], S("lncol")[:, lnb + 4:lnb + 5],
                EPS, None, ALU.add))
            k.wait("scalar", "dv")
            k.tiny("scalar", "ac", lambda e, lnb=lnb, lbB=lbB: e.activation(
                S("lncolB")[:, lbB + 0:lbB + 1], S("lncol")[:, lnb + 5:lnb + 6],
                AF.Sqrt))
            k.wait("vector", "ac")
            k.tiny("vector", "dv", lambda e, lnb=lnb, lbB=lbB: e.reciprocal(
                S("lncol")[:, lnb + 6:lnb + 7], S("lncolB")[:, lbB + 0:lbB + 1]))
            k.tiny("vector", "dv", lambda e, lnb=lnb: e.tensor_scalar(
                S("lncol")[:, lnb + 7:lnb + 8], S("lncol")[:, lnb + 1:lnb + 2],
                S("lncol")[:, lnb + 6:lnb + 7], None, ALU.mult))
            k.inc("vector", "dv", 1, lambda e, bi=bi, lnb=lnb: e.tensor_scalar(
                S("xnf"), xt[bi], S("lncol")[:, lnb + 6:lnb + 7], S("lncol")[:, lnb + 7:lnb + 8],
                ALU.mult, ALU.subtract))
            buf_free[bi] = k.cnt["dv"]
            k.inc("vector", "dv", 1, lambda e: e.tensor_copy(S("xnb"), S("xnf")))
            k.inc("vector", "dv", 1, lambda e: e.scalar_tensor_tensor(
                S("xlo"), S("xnf"), 1.0, S("xnb"), ALU.mult, ALU.subtract))
            dvm = k.mark("dv")
            k.wait("sync", "dv", dvm)
            k.inc("sync", "qsy", 16, lambda e, tc=tc: e.dma_start(
                out=xnT3[:, :, tc * 128:(tc + 1) * 128],
                in_=S("xnb").rearrange("p (kc d) -> p kc d", kc=8), transpose=True))
            qym = k.inc("sync", "qsy", 16, lambda e, tc=tc: e.dma_start(
                out=xloT3[:, :, tc * 128:(tc + 1) * 128],
                in_=S("xlo").rearrange("p (kc d) -> p kc d", kc=8), transpose=True))
            if not hasattr(k, "ph1_t_marks"):
                k.ph1_t_marks = []
            k.ph1_t_marks.append(qym)
            k.wait("vector", "qsy", qym)

            if SKIP_ROUTER:
                continue
            k.wait("tensor", "qsy", qym)
            if tc == 0:
                k.wait("tensor", "dv", k.rhi_dv)
            k.bank_guard(7)
            nmm = 0
            for (lh, rh) in ((S("xnT"), S("rhi")), (S("xnT"), S("rlo")),
                             (S("xnloT"), S("rhi"))):
                for kc in range(8):
                    first = nmm == 0
                    last = nmm == 23
                    k.inc("tensor", "pe", 1, lambda e, lh=lh, rh=rh, kc=kc, tc=tc,
                          first=first, last=last: e.matmul(
                        k.bank(7, NE),
                        lh.rearrange("p (kc t) -> p kc t", kc=8)[:, kc, tc * 128:(tc + 1) * 128],
                        rh[:, kc * NE:(kc + 1) * NE],
                        start=first, stop=last))
                    nmm += 1
            pem = k.mark("pe")
            k.wait("vector", "pe", pem)
            k.tiny("vector", "dv", lambda e: e.reduce_max(
                S("rt")[:, 0:1], k.bank(7, NE), axis=mybir.AxisListType.X))
            k.tiny("vector", "dv", lambda e: e.tensor_scalar(
                S("rt")[:, 1:2], S("rt")[:, 0:1], -1.0, None, ALU.mult))
            k.wait("scalar", "dv")
            k.tiny("scalar", "ac", lambda e: e.activation(
                S("rte"), k.bank(7, NE), AF.Exp, bias=S("rt")[:, 1:2]))
            acm_r = k.mark("ac")
            k.bank_consumed(7, "ac")
            k.wait("vector", "ac", acm_r)
            k.tiny("vector", "dv", lambda e: e.reduce_sum(
                S("rt")[:, 2:3], S("rte"), axis=mybir.AxisListType.X))
            k.tiny("vector", "dv", lambda e: e.reciprocal(
                S("rt")[:, 3:4], S("rt")[:, 2:3]))
            k.tiny("vector", "dv", lambda e: e.tensor_scalar(
                S("rtp"), S("rte"), S("rt")[:, 3:4], None, ALU.mult))
            k.tiny("vector", "dv", lambda e: e.reduce_max(
                S("rt")[:, 4:5], S("rtp"), axis=mybir.AxisListType.X))
            k.tiny("vector", "dv", lambda e: e.tensor_scalar(
                S("rteq1"), S("rtp"), S("rt")[:, 4:5], None, ALU.is_ge))
            k.tiny("vector", "dv", lambda e: e.scalar_tensor_tensor(
                S("rtq"), S("rteq1"), -BIG, S("rtp"), ALU.mult, ALU.add))
            k.tiny("vector", "dv", lambda e: e.reduce_max(
                S("rt")[:, 5:6], S("rtq"), axis=mybir.AxisListType.X))
            k.tiny("vector", "dv", lambda e: e.tensor_scalar(
                S("rteq2"), S("rtq"), S("rt")[:, 5:6], None, ALU.is_ge))
            k.tiny("vector", "dv", lambda e: e.tensor_scalar(
                S("rt")[:, 8:9], S("rt")[:, 4:5], S("rt")[:, 5:6], None, ALU.subtract))
            k.wait("scalar", "dv")
            k.tiny("scalar", "ac", lambda e: e.activation(
                S("rtB")[:, 0:1], S("rt")[:, 8:9], AF.Sigmoid))
            k.wait("vector", "ac")
            k.tiny("vector", "dv", lambda e: e.tensor_scalar(
                S("rt")[:, 7:8], S("rtB")[:, 0:1], -1.0, 1.0, ALU.mult, ALU.add))
            k.tiny("vector", "dv", lambda e: e.tensor_scalar(
                S("rteq1"), S("rteq1"), S("rtB")[:, 0:1], None, ALU.mult))
            k.inc("vector", "dv", 1, lambda e, tc=tc: e.scalar_tensor_tensor(
                S("fullw")[:, tc * NE:(tc + 1) * NE], S("rteq2"), S("rt")[:, 7:8],
                S("rteq1"), ALU.mult, ALU.add))
        k.tap("xnT", lambda: S("xnT"), [128, 8 * T], BF16, waits=("qsy",))
        k.tap("xnloT", lambda: S("xnloT"), [128, 8 * T], BF16, waits=("qsy",))
        k.tap("fullw", lambda: S("fullw"), [128, 128], F32, waits=("dv",))
        k.tap("lncol", lambda: S("lncol"), [128, 128], F32, waits=("dv", "ac"))
        k.tap("lncolB", lambda: S("lncolB"), [128, 64], F32, waits=("dv", "ac"))

    if BUILD_UPTO >= 1:
        ph1()
        phases_built.append(1)

    p2, p2_end = carve(nc, es, TB, [
        ("xBC", [128, 18 * TPAD], BF16),
        ("dtaln_f", [64, T], F32), ("dtaln_r", [64, T], F32),
        ("dt_pre", [NH, T], F32),
        ("wdc0", [128, 1024], BF16), ("wdc1", [128, 1024], BF16),
    ])
    k.sb.update(p2)
    k.conv_scratch_base = p2_end

    def ph2():
        xBC3 = S("xBC").rearrange("p (j c) -> p j c", c=TPAD)
        wdc = [S("wdc0"), S("wdc1")]
        wq_ready = {}

        def emit_wload(dc, wait_pe):
            bi = dc % 2
            cols = 128 if dc < 18 else NH
            k.wait("gpsimd", "pe", wait_pe)
            wq_ready[dc] = k.inc("gpsimd", "qgp", 16,
                lambda e, dc=dc, bi=bi, cols=cols: e.dma_start(
                    out=wdc[bi][:, 0:8 * cols].rearrange("p (kc m) -> p kc m", kc=8),
                    in_=DI["winT"][:, DIN + dc * 128: DIN + dc * 128 + cols]
                        .rearrange("(kc p) m -> p kc m", p=128)))

        emit_wload(0, 0)
        emit_wload(1, 0)
        ldq = k.mark("qsy")
        tmarks = getattr(k, "ph1_t_marks", None)
        k.wait("tensor", "qsy", ldq)
        for dc in range(19):
            bi = dc % 2
            cols = 128 if dc < 18 else NH
            k.wait("tensor", "qgp", wq_ready[dc])
            for b in range(4):
                k.bank_guard(b)
            for th2 in range(2):
                if tmarks is not None:
                    k.wait("tensor", "qsy", tmarks[8 * (th2 + 1) - 1])
                for kc in range(8):
                    for tsl in (2 * th2, 2 * th2 + 1):
                        k.inc("tensor", "pe", 1, lambda e, dc=dc, bi=bi, tsl=tsl,
                              kc=kc, cols=cols: e.matmul(
                            k.bank(tsl, 512)[0:cols if cols < 128 else 128, :],
                            wdc[bi][:, kc * cols:(kc + 1) * cols],
                            S("xnT").rearrange("p (kc t) -> p kc t", kc=8)[:, kc, tsl * 512:(tsl + 1) * 512],
                            start=(kc == 0), stop=(kc == 7)))
            for tsl in range(4):
                b = tsl
                pem = k.mark("pe")
                if dc < 18:
                    k.wait("scalar", "pe", pem)
                    k.inc("scalar", "ac", 1, lambda e, dc=dc, tsl=tsl, b=b: e.activation(
                        xBC3[:, dc, 4 + tsl * 512: 4 + (tsl + 1) * 512],
                        k.bank(b, 512), AF.Copy))
                    k.bank_consumed(b, "ac")
                else:
                    k.wait("vector", "pe", pem)
                    k.inc("vector", "dv", 1, lambda e, tsl=tsl, b=b: e.tensor_copy(
                        S("dt_pre")[:, tsl * 512:(tsl + 1) * 512],
                        k.bank(b, 512)[0:NH, :]))
                    k.bank_consumed(b, "dv")
            if dc + 2 < 19:
                emit_wload(dc + 2, k.mark("pe"))
        k.inc("vector", "dv", 1, lambda e: e.memset(xBC3[:, :, 0:4], 0.0))
        k.inc("vector", "dv", 1, lambda e: e.memset(xBC3[:, :, 2052:2056], 0.0))
        k.tap("dt_pre", lambda: S("dt_pre"), [NH, T], F32, waits=("dv",))

    if BUILD_UPTO >= 2:
        ph2()
        phases_built.append(2)

    p3, _ = carve(nc, es, CARVE_OFFS["wdc0"], [
        ("sndEs", [128, 128], BF16), ("rcvEs", [128, 128], BF16),
    ])
    k.conv_scratch_base = CARVE_OFFS["dt_pre"]
    k.sb.update(p3)

    def ph3():
        xBC3 = S("xBC").rearrange("p (j c) -> p j c", c=TPAD)
        k.wait("vector", "ac")
        k.wait("vector", "dv")
        k.inc("vector", "dv", 1, lambda e: e.memset(S("sndEs"), 0.0))
        k.inc("vector", "dv", 1, lambda e: e.tensor_scalar(
            S("sndEs")[:, 0:54].rearrange("p (j c) -> p j c", c=3),
            xBC3[:, :, 2049:2052], S("flags")[:, 4:5], None, ALU.mult))
        k.inc("vector", "dv", 1, lambda e: e.tensor_scalar(
            S("sndEs")[:, 64:118].rearrange("p (j c) -> p j c", c=3),
            xBC3[:, :, 4:7], S("flags")[:, 5:6], None, ALU.mult))
        dvm = k.mark("dv")
        k.wait("scalar", "dv", dvm)
        qm = k.inc("scalar", "qsc", 16, lambda e: e.dma_start(
            out=k.dr["sndE"].ap(), in_=S("sndEs")))
        k.wait("gpsimd", "qsc", qm)
        k.inc("gpsimd", "cc", 1, lambda e: e.collective_compute(
            "AllReduce", ALU.add,
            replica_groups=[[0, 1], [2, 3], [4, 5], [6, 7]],
            ins=[k.dr["sndE"].ap().opt()], outs=[k.dr["rcvE"].ap().opt()]))
        ccm = k.mark("cc")
        k.wait("scalar", "cc", ccm)
        qm = k.inc("scalar", "qsc", 16, lambda e: e.dma_start(
            out=S("rcvEs"), in_=k.dr["rcvE"].ap()))
        k.wait("vector", "qsc", qm)
        k.inc("vector", "dv", 1, lambda e: e.tensor_scalar(
            xBC3[:, :, 1:4], S("rcvEs")[:, 0:54].rearrange("p (j c) -> p j c", c=3),
            S("flags")[:, 6:7], None, ALU.mult))
        k.inc("vector", "dv", 1, lambda e: e.tensor_scalar(
            xBC3[:, :, 2052:2055],
            S("rcvEs")[:, 64:118].rearrange("p (j c) -> p j c", c=3),
            S("flags")[:, 7:8], None, ALU.mult))
        k.xbc_patched = k.mark("dv")
        k.tap("xBC", lambda: S("xBC"), [128, 18 * TPAD], BF16, waits=("dv",))

    if BUILD_UPTO >= 3:
        ph3()
        phases_built.append(3)

    def ph4():
        for di, d in enumerate(("f", "r")):
            a = S(f"dtaln_{d}")[0:NH, :]
            b = S(f"dtaln_{d}")[NH:2 * NH, :]
            c = S("dt_pre")
            k.wait("vector", "dv")
            k.wait("vector", "ac")
            if di == 0:
                k.inc("vector", "dv", 1, lambda e: e.tensor_scalar(
                    S("dtaln_r")[0:NH, :], S("dt_pre"), S("dtbcol_r"), None, ALU.add))
                k.inc("vector", "dv", 1, lambda e, a=a: e.tensor_scalar(
                    a, S("dt_pre"), S("dtbcol_f"), None, ALU.add))
            dvm = k.mark("dv")
            k.wait("scalar", "dv", dvm)
            k.inc("scalar", "ac", 1, lambda e, a=a, c=c: e.activation(
                c, a, AF.Abs))
            k.inc("scalar", "ac", 1, lambda e, c=c: e.activation(
                c, c, AF.Exp, scale=-1.0))
            k.inc("scalar", "ac", 1, lambda e, c=c: e.activation(
                c, c, AF.Ln, bias=1.0))
            acm = k.mark("ac")
            k.wait("vector", "ac", acm)
            k.inc("vector", "dv", 1, lambda e, a=a: e.tensor_scalar(
                a, a, 0.0, None, ALU.max))
            k.inc("vector", "dv", 1, lambda e, a=a, c=c: e.tensor_tensor(
                a, a, c, ALU.add))
            dvm = k.mark("dv")
            k.wait("scalar", "dv", dvm)
            k.inc("scalar", "ac", 1, lambda e, a=a, b=b: e.activation(
                b, a, AF.Ln))
            k.wait("vector", "ac")
            k.inc("vector", "dv", 1, lambda e, a=a, d=d: e.tensor_scalar(
                a, a, S(f"nacol_{d}"), None, ALU.mult))
            k.tap(f"dtaln_{d}", lambda d=d: S(f"dtaln_{d}"), [64, T], F32,
                  waits=("dv", "ac"))

    if BUILD_UPTO >= 4:
        ph4()
        phases_built.append(4)

    p5, p5_end = carve(nc, es, k.conv_scratch_base, [
        ("fmA", [128, T], BF16), ("fmB", [128, T], BF16), ("fmC", [128, T], BF16),
    ])
    k.sb.update(p5)
    p5b, _ = carve(nc, es, TB, [
        ("xsj0", [128, DIN], BF16), ("xsj1", [128, DIN], BF16),
        ("xds", [128, DIN], BF16), ("slam", [128, DIN], BF16),
        ("t64s", [128, 64], F32), ("acumT", [128, 32], F32),
        ("dsc", [128, 32], F32),
    ])
    k.sb.update(p5b)

    def conv_dir(d):
        xBC3 = S("xBC").rearrange("p (j c) -> p j c", c=TPAD)
        if d == "f":
            taps = [(3, 4, False), (1, 2, False), (0, 1, False), (2, 3, False)]
        else:
            taps = [(3, 4, False), (1, 6, False), (0, 7, False), (2, 5, False)]
        fm_free_ac = getattr(k, "conv_fm_free_ac", 0)
        fmB_free = getattr(k, "conv_fmB_free", 0)
        fmC_free = getattr(k, "conv_fmC_free", 0)
        for j in range(18):
            k.wait("vector", "dv", k.xbc_patched)
            k.wait("vector", "ac", fm_free_ac)
            kk, off, _u = taps[0]
            k.inc("vector", "dv", 1, lambda e, j=j, kk=kk, off=off: e.tensor_scalar(
                S("fmA"), xBC3[:, j, off:off + T],
                S(f"convw_{d}")[:, j * 4 + kk:j * 4 + kk + 1],
                S(f"convb_{d}")[:, j:j + 1], ALU.mult, ALU.add))
            for kk, off, _u in taps[1:]:
                k.inc("vector", "dv", 1, lambda e, j=j, kk=kk, off=off:
                      e.scalar_tensor_tensor(
                          S("fmA"), xBC3[:, j, off:off + T],
                          S(f"convw_{d}")[:, j * 4 + kk:j * 4 + kk + 1],
                          S("fmA"), ALU.mult, ALU.add))
            dvm = k.mark("dv")
            if j < 16:
                dest = S("fmB")
            elif j == 16:
                dest = S(f"bcfm_{d}")[:, 0:T]
            else:
                dest = S(f"bcfm_{d}")[:, T:2 * T]
            k.wait("scalar", "dv", dvm)
            if j < 16:
                k.wait("scalar", "qsy", fmB_free)
            fm_free_ac = k.inc("scalar", "ac", 1, lambda e, dest=dest: e.activation(
                dest, S("fmA"), AF.Silu))
            if j < 16:
                k.wait("sync", "ac")
                k.wait("sync", "qsc", fmC_free)
                fmB_free = k.inc("sync", "qsy", 16, lambda e: e.dma_start(
                    out=S("fmC").rearrange("p (tc c) -> p tc c", c=128),
                    in_=S("fmB").rearrange("f (tc t) -> f tc t", tc=NCHUNK),
                    transpose=True))
                k.wait("scalar", "qsy")
                fmC_free = k.inc("scalar", "qsc", 16, lambda e, j=j, d=d: e.dma_start(
                    out=k.dr[f"xstd_{d}"].ap().rearrange(
                        "p (tc m) -> p tc m", m=DIN)[:, :, j * 128:(j + 1) * 128],
                    in_=S("fmC").rearrange("p (tc c) -> p tc c", c=128)))
            elif j == 16:
                k.wait("sync", "ac")
                k.inc("sync", "qsy", 16, lambda e, d=d: e.dma_start(
                    out=S(f"btok_{d}").rearrange("p (tc c) -> p tc c", c=128),
                    in_=S(f"bcfm_{d}")[:, 0:T].rearrange(
                        "f (tc t) -> f tc t", tc=NCHUNK), transpose=True))
        k.conv_fm_free_ac = fm_free_ac
        k.conv_fmB_free = fmB_free
        k.conv_fmC_free = fmC_free

    def pre_dir(d):
        order = range(NCHUNK) if d == "f" else range(NCHUNK - 1, -1, -1)
        ul = S("ulc") if d == "f" else S("urc")
        sel = S("sel127") if d == "f" else S("sel0")
        dtaln = S(f"dtaln_{d}")
        xsj = [S("xsj0"), S("xsj1")]
        k.wait("vector", "qsc")
        k.inc("vector", "dv", 1, lambda e: e.memset(S("s0t"), 0.0))
        xsj_free = [0, 0]
        for i, tc in enumerate(order):
            bi = i % 2
            k.wait("scalar", "qsc")
            k.wait("scalar", "dv", xsj_free[bi])
            qm = k.inc("scalar", "qsc", 16, lambda e, tc=tc, bi=bi, d=d: e.dma_start(
                out=xsj[bi], in_=k.dr[f"xstd_{d}"].ap()[:, tc * DIN:(tc + 1) * DIN]))
            k.bank_guard(0)
            k.inc("tensor", "pe", 1, lambda e, tc=tc, dtaln=dtaln: e.transpose(
                k.bank(0, 64), dtaln[:, tc * 128:(tc + 1) * 128],
                S("idf32")[0:64, 0:64]))
            pem = k.mark("pe")
            k.wait("vector", "pe", pem)
            k.tiny("vector", "dv", lambda e: e.tensor_copy(S("t64s"), k.bank(0, 64)))
            k.wait("tensor", "dv")
            k.inc("tensor", "pe", 1, lambda e, ul=ul: e.matmul(
                k.bank(0, 32, off=64), ul, S("t64s")[:, 0:32],
                start=True, stop=True))
            k.inc("tensor", "pe", 1, lambda e, ul=ul: e.matmul(
                k.bank(0, 128, off=96)[0:32, :], S("t64s")[:, 0:32], ul,
                start=True, stop=True))
            pem = k.mark("pe")
            k.wait("vector", "pe", pem)
            k.tiny("vector", "dv", lambda e: e.tensor_copy(
                S("acumT"), k.bank(0, 32, off=64)))
            k.wait("scalar", "pe", pem)
            k.wait("scalar", "dv")
            k.tiny("scalar", "ac", lambda e, tc=tc, d=d: e.activation(
                S(f"stack96_{d}")[0:32, tc * 128:(tc + 1) * 128],
                k.bank(0, 128, off=96)[0:32, :], AF.Copy))
            k.wait("vector", "ac")
            k.tiny("vector", "dv", lambda e, tc=tc, d=d: e.scalar_tensor_tensor(
                S(f"stack96_{d}")[32:64, tc * 128:(tc + 1) * 128],
                k.bank(0, 128, off=96)[0:32, :], 1.0,
                S(f"stack96_{d}")[0:32, tc * 128:(tc + 1) * 128],
                ALU.mult, ALU.subtract))
            k.tiny("vector", "dv", lambda e, tc=tc, d=d, dtaln=dtaln: e.tensor_copy(
                S(f"stack96_{d}")[64:96, tc * 128:(tc + 1) * 128],
                dtaln[32:64, tc * 128:(tc + 1) * 128]))
            k.wait("tensor", "dv")
            k.inc("tensor", "pe", 1, lambda e, sel=sel: e.matmul(
                k.bank(0, 32, off=224), sel, S("acumT"), start=True, stop=True))
            pem = k.mark("pe")
            k.wait("scalar", "pe", pem)
            k.tiny("scalar", "ac", lambda e, tc=tc, d=d: e.activation(
                S(f"lbce_{d}")[:, tc * 32:(tc + 1) * 32],
                k.bank(0, 32, off=224), AF.Exp))
            k.tiny("vector", "dv", lambda e: e.scalar_tensor_tensor(
                S("dsc"), S("t64s")[:, 32:64], 1.0, S("acumT"),
                ALU.mult, ALU.subtract))
            k.wait("vector", "pe", pem)
            k.tiny("vector", "dv", lambda e: e.tensor_tensor(
                S("dsc"), S("dsc"), k.bank(0, 32, off=224), ALU.add))
            k.bank_consumed(0, "dv")
            k.wait("scalar", "dv")
            k.tiny("scalar", "ac", lambda e, tc=tc, d=d: e.activation(
                S(f"ddt_{d}")[:, tc * 32:(tc + 1) * 32], S("dsc"), AF.Exp))
            k.wait("vector", "qsc", qm)
            k.wait("vector", "ac")
            k.inc("vector", "dv", 1, lambda e, tc=tc, bi=bi, d=d: e.tensor_tensor(
                S("xds").rearrange("p (h q) -> p h q", h=NH),
                xsj[bi].rearrange("p (h q) -> p h q", h=NH),
                S(f"ddt_{d}")[:, tc * 32:(tc + 1) * 32]
                    .unsqueeze(2).broadcast_to([128, NH, HD]), ALU.mult))
            xsj_free[bi] = k.cnt["dv"]
            dvm = k.mark("dv")
            k.wait("tensor", "dv", dvm)
            for q in range(4):
                k.bank_guard(4 + q)
                k.inc("tensor", "pe", 1, lambda e, q=q, tc=tc, d=d: e.matmul(
                    k.bank(4 + q, 512), S(f"btok_{d}")[:, tc * 128:(tc + 1) * 128],
                    S("xds")[:, q * 512:(q + 1) * 512], start=True, stop=True))
            pem = k.mark("pe")
            k.inc("vector", "dv", 1, lambda e, tc=tc, d=d: e.tensor_tensor(
                S("slam").rearrange("p (h q) -> p h q", h=NH),
                S("s0t").rearrange("p (h q) -> p h q", h=NH),
                S(f"lbce_{d}")[:, tc * 32:(tc + 1) * 32]
                    .unsqueeze(2).broadcast_to([128, NH, HD]), ALU.mult))
            k.wait("vector", "pe", pem)
            k.inc("vector", "dv", 1, lambda e: e.tensor_tensor(
                S("s0t"), k.ps.ap()[:, 2048:4096], S("slam"), ALU.add))
            for q in range(4):
                k.bank_consumed(4 + q, "dv")
        fcol = 0 if d == "f" else 1
        k.inc("vector", "dv", 1, lambda e, fcol=fcol: e.tensor_scalar(
            S("slam"), S("s0t"), S("flags")[:, fcol:fcol + 1], None, ALU.mult))
        dvm = k.mark("dv")
        k.wait("scalar", "dv", dvm)
        k.inc("scalar", "qsc", 16, lambda e, d=d: e.dma_start(
            out=k.dr["sndS"].ap()[:, (0 if d == "f" else DIN):
                                  (DIN if d == "f" else 2 * DIN)],
            in_=S("slam")))

    def ph5():
        conv_dir("f")
        conv_dir("r")
        k.barrier()
        pre_dir("f")
        pre_dir("r")
        k.tap("bcfm_f", lambda: S("bcfm_f"), [128, 2 * T], BF16, waits=("ac",))
        k.tap("btok_f", lambda: S("btok_f"), [128, T], BF16, waits=("qsy",))
        k.tap("stack96_f", lambda: S("stack96_f"), [96, NCHUNK * 128], BF16,
              waits=("dv", "ac"))
        k.tap("stack96_r", lambda: S("stack96_r"), [96, NCHUNK * 128], BF16,
              waits=("dv", "ac"))
        k.tap("lbce_f", lambda: S("lbce_f"), [128, NCHUNK * NH], BF16,
              waits=("ac",))
        k.tap("ddt_f", lambda: S("ddt_f"), [128, NCHUNK * NH], BF16,
              waits=("ac",))

    if BUILD_UPTO >= 5:
        ph5()
        phases_built.append(5)

    def ph6():
        k.wait("gpsimd", "qsc")
        k.inc("gpsimd", "cc", 1, lambda e: e.collective_compute(
            "AllReduce", ALU.add,
            replica_groups=[[0, 1], [2, 3], [4, 5], [6, 7]],
            ins=[k.dr["sndS"].ap().opt()], outs=[k.dr["rcvS"].ap().opt()]))

    if BUILD_UPTO >= 6:
        ph6()
        phases_built.append(6)

    p7, _ = carve(nc, es, TB, [
        ("wz", [128, 8 * DIN], BF16),
        ("szb0", [128, DIN], BF16), ("szb1", [128, DIN], BF16),
    ])
    k.sb.update(p7)

    def ph7():
        k.barrier()
        wzq = k.inc("gpsimd", "qgp", 16, lambda e: e.dma_start(
            out=S("wz").rearrange("p (kc m) -> p kc m", kc=8),
            in_=DI["winT"][:, 0:DIN].rearrange("(kc p) m -> p kc m", p=128)))
        k.wait("tensor", "qgp", wzq)
        szb = [S("szb0"), S("szb1")]
        szb_free = [0, 0]
        xnT3 = S("xnT").rearrange("p (kc t) -> p kc t", kc=8)
        for tc in range(NCHUNK):
            bi = tc % 2
            bank0 = 0 if bi == 0 else 4
            for zsl in range(4):
                k.bank_guard(bank0 + zsl)
            for kc in range(8):
                for zsl in range(4):
                    k.inc("tensor", "pe", 1, lambda e, tc=tc, zsl=zsl, kc=kc,
                          b=bank0 + zsl: e.matmul(
                        k.bank(b, 512),
                        xnT3[:, kc, tc * 128:(tc + 1) * 128],
                        S("wz")[:, kc * DIN + zsl * 512: kc * DIN + (zsl + 1) * 512],
                        start=(kc == 0), stop=(kc == 7)))
            pem = k.mark("pe")
            for zsl in range(4):
                k.wait("scalar", "pe", pem)
                k.wait("scalar", "qsc", szb_free[bi])
                k.inc("scalar", "ac", 1, lambda e, zsl=zsl, bi=bi,
                      b=bank0 + zsl: e.activation(
                    szb[bi][:, zsl * 512:(zsl + 1) * 512], k.bank(b, 512), AF.Silu))
                k.bank_consumed(bank0 + zsl, "ac")
            acm = k.mark("ac")
            k.wait("scalar", "ac", acm)
            szb_free[bi] = k.inc("scalar", "qsc", 16, lambda e, tc=tc, bi=bi:
                                 e.dma_start(
                out=k.dr["siluzd"].ap()[:, tc * DIN:(tc + 1) * DIN], in_=szb[bi]))
        k.wait("scalar", "cc")
        q1 = k.inc("scalar", "qsc", 16, lambda e: e.dma_start(
            out=S("szb0"), in_=k.dr["rcvS"].ap()[:, 0:DIN]))
        q2 = k.inc("scalar", "qsc", 16, lambda e: e.dma_start(
            out=S("szb1"), in_=k.dr["rcvS"].ap()[:, DIN:2 * DIN]))
        k.wait("vector", "qsc", q2)
        k.inc("vector", "dv", 1, lambda e: e.tensor_scalar(
            S("sin_f"), S("szb0"), S("flags")[:, 2:3], None, ALU.mult))
        k.inc("vector", "dv", 1, lambda e: e.tensor_scalar(
            S("sin_r"), S("szb1"), S("flags")[:, 3:4], None, ALU.mult))
        k.tap("sin_f", lambda: S("sin_f"), [128, DIN], BF16, waits=("dv",))
        k.tap("sin_r", lambda: S("sin_r"), [128, DIN], BF16, waits=("dv",))

    if BUILD_UPTO >= 7:
        ph7()
        phases_built.append(7)

    p8, p8_end = carve(nc, es, TB, [
        ("Ebuf", [128, 4096], BF16), ("aEbuf", [128, 4096], BF16),
        ("rowbuf", [2, 4096], BF16),
        ("mxs0", [128, DIN], BF16), ("mxs1", [128, DIN], BF16),
        ("msz0", [128, DIN], BF16), ("msz1", [128, DIN], BF16),
        ("my30", [128, DIN], BF16), ("my31", [128, DIN], BF16),
        ("mg1", [128, 1024], BF16), ("mg2", [128, DIN], BF16),
        ("myg", [128, DIN], BF16), ("msq", [128, DIN], BF16),
        ("mxds", [128, DIN], BF16), ("mslam", [128, DIN], BF16),
        ("mysb0", [128, DIN], BF16), ("mysb1", [128, DIN], BF16),
        ("mscoresT", [128, 128], BF16),
        ("gcol", [128, 8], F32), ("gcolB", [128, 4], F32),
        ("nblf", [128, 512], BF16), ("nblr", [128, 512], BF16),
    ])
    k.sb.update(p8)

    def main_dir(d):
        order = range(NCHUNK) if d == "f" else range(NCHUNK - 1, -1, -1)
        nbl = S("nblf") if d == "f" else S("nblr")
        xsj = [S("mxs0"), S("mxs1")]
        szj = [S("msz0"), S("msz1")]
        y3j = [S("my30"), S("my31")]
        ysb = [S("mysb0"), S("mysb1")]
        xsj_free = getattr(k, "m_xsj_free", [0, 0])
        szj_free = getattr(k, "m_szj_free", [0, 0])
        y3j_free = getattr(k, "m_y3j_free", [0, 0])
        ysb_free = getattr(k, "m_ysb_free", [0, 0])
        k.wait("vector", "dv")
        k.wait("vector", "pe")
        k.inc("vector", "dv", 1, lambda e, d=d: e.tensor_copy(
            S("s0t"), S(f"sin_{d}")))
        rowbuf_free = getattr(k, "m_rowbuf_free", 0)

        def emit_block(cx):
            tc, bi = cx["tc"], cx["bi"]
            plm = cx["plm"]; dvm_m2 = cx["dvm_m2"]
            for hh in range(2):
                k.bank_guard(4); k.bank_guard(5)
                if hh == 0:
                    k.wait("tensor", "dv", cx["m2h_dv"])
                    k.wait("tensor", "pl", cx["m2h_pl"])
                else:
                    k.wait("tensor", "dv", dvm_m2)
                    k.wait("tensor", "pl", plm)
                for hl in range(16):
                    hg = hh * 16 + hl
                    bnk = 4 + hl // 8
                    k.inc("tensor", "pe", 1, lambda e, hg=hg, hl=hl, bnk=bnk,
                          bi=bi: e.matmul(
                        k.bank(bnk, 64, off=(hl % 8) * 64),
                        S("Ebuf")[:, hg * 128:(hg + 1) * 128],
                        xsj[bi][:, hg * 64:(hg + 1) * 64],
                        start=(hl % 8 == 0), stop=False))
                for hl in range(16):
                    hg = hh * 16 + hl
                    bnk = 4 + hl // 8
                    k.inc("tensor", "pe", 1, lambda e, hg=hg, hl=hl,
                          bnk=bnk: e.matmul(
                        k.bank(bnk, 64, off=(hl % 8) * 64),
                        S("aEbuf")[:, hg * 128:(hg + 1) * 128],
                        S("s0t")[:, hg * 64:(hg + 1) * 64],
                        start=False, stop=(hl % 8 == 7)))
                k.bank_guard(6); k.bank_guard(7)
                k.wait("tensor", "pl", cx["pl_xds"])
                for q in range(2):
                    k.inc("tensor", "pe", 1, lambda e, q=q, hh=hh, tc=tc,
                          d=d: e.matmul(
                        k.bank(6 + q, 512),
                        S(f"btok_{d}")[:, tc * 128:(tc + 1) * 128],
                        S("mxds")[:, hh * 1024 + q * 512:hh * 1024 + (q + 1) * 512],
                        start=True, stop=True))
                pem = k.mark("pe")
                k.wait("gpsimd", "qsc", cx["qx"])
                k.wait("gpsimd", "dv")
                k.inc("gpsimd", "pl", 1, lambda e, hh=hh, bi=bi, d=d: e.tensor_tensor(
                    S("mg1").rearrange("p (h q) -> p h q", h=16),
                    xsj[bi][:, hh * 1024:(hh + 1) * 1024]
                        .rearrange("p (h q) -> p h q", h=16),
                    S(f"dbc_{d}")[:, hh * 16:(hh + 1) * 16]
                        .unsqueeze(2).broadcast_to([128, 16, HD]), ALU.mult))
                k.wait("vector", "pe", pem)
                k.wait("vector", "pl")
                k.wait("vector", "qsc", getattr(k, "mg2_hold", 0))
                k.inc("vector", "dv", 1, lambda e, hh=hh: e.scalar_tensor_tensor(
                    S("mg2")[:, hh * 1024:(hh + 1) * 1024],
                    k.ps.ap()[:, 4 * 512:6 * 512], 1.0, S("mg1"),
                    ALU.mult, ALU.add))
                k.bank_consumed(4, "dv"); k.bank_consumed(5, "dv")
                k.wait("gpsimd", "pe", pem)
                k.inc("gpsimd", "pl", 1, lambda e, hh=hh, tc=tc, d=d: e.tensor_tensor(
                    S("mslam")[:, hh * 1024:(hh + 1) * 1024]
                        .rearrange("p (h q) -> p h q", h=16),
                    S("s0t")[:, hh * 1024:(hh + 1) * 1024]
                        .rearrange("p (h q) -> p h q", h=16),
                    S(f"lbce_{d}")[:, tc * 32 + hh * 16:tc * 32 + (hh + 1) * 16]
                        .unsqueeze(2).broadcast_to([128, 16, HD]), ALU.mult))
                k.wait("vector", "pl")
                k.inc("vector", "dv", 1, lambda e, hh=hh: e.tensor_tensor(
                    S("s0t")[:, hh * 1024:(hh + 1) * 1024],
                    k.ps.ap()[:, 6 * 512:8 * 512],
                    S("mslam")[:, hh * 1024:(hh + 1) * 1024], ALU.add))
                k.bank_consumed(6, "dv"); k.bank_consumed(7, "dv")
            k.mm_prev_ydyo = k.mark("pe")
            k.wait("vector", "qsc", cx["qz"])
            k.inc("vector", "dv", 1, lambda e, bi=bi: e.tensor_tensor(
                S("myg"), S("mg2"), szj[bi], ALU.mult))
            szj_free[bi] = k.cnt["dv"]
            k.wait("vector", "qsc", getattr(k, "msq_hold", 0))
            k.inc("vector", "dv", 1, lambda e: e.tensor_tensor(
                S("msq"), S("myg"), S("myg"), ALU.mult))
            k.tiny("vector", "dv", lambda e: e.reduce_sum(
                S("gcol")[:, 0:1], S("msq"), axis=mybir.AxisListType.X))
            k.tiny("vector", "dv", lambda e: e.tensor_scalar(
                S("gcol")[:, 1:2], S("gcol")[:, 0:1], 1.0 / DIN, EPS,
                ALU.mult, ALU.add))
            k.wait("scalar", "dv")
            k.tiny("scalar", "ac", lambda e: e.activation(
                S("gcolB")[:, 0:1], S("gcol")[:, 1:2], AF.Sqrt))
            k.wait("vector", "ac")
            k.tiny("vector", "dv", lambda e: e.reciprocal(
                S("gcol")[:, 2:3], S("gcolB")[:, 0:1]))
            k.inc("vector", "dv", 1, lambda e, d=d: e.scalar_tensor_tensor(
                S("msq"), S("myg"), S("gcol")[:, 2:3], S(f"nwbc_{d}"),
                ALU.mult, ALU.mult))
            xsj_free[bi] = k.cnt["dv"]
            if d == "f":
                dvm = k.mark("dv")
                k.wait("scalar", "dv", dvm)
                k.msq_hold = k.inc("scalar", "qsc", 16, lambda e, tc=tc: e.dma_start(
                    out=k.dr["y3fd"].ap()[:, tc * DIN:(tc + 1) * DIN],
                    in_=S("msq")))
            else:
                k.wait("vector", "qsc", cx["qy3"])
                k.wait("vector", "qsc", ysb_free[bi])
                k.inc("vector", "dv", 1, lambda e, bi=bi: e.tensor_tensor(
                    ysb[bi], S("msq"), y3j[bi], ALU.add))
                y3j_free[bi] = k.cnt["dv"]
                dvm = k.mark("dv")
                k.wait("scalar", "dv", dvm)
                ysb_free[bi] = k.inc("scalar", "qsc", 16, lambda e, tc=tc,
                                     bi=bi: e.dma_start(
                    out=k.dr["ysumd"].ap()[:, tc * DIN:(tc + 1) * DIN],
                    in_=ysb[bi]))
            if f"ypred_{d}" in DEBUG_TAPS:
                dvm = k.mark("dv")
                k.wait("scalar", "dv", dvm)
                k.mg2_hold = k.inc("scalar", "qsc", 16,
                                   lambda e, tc=tc, d=d: e.dma_start(
                    out=k.dr[f"ypred_{d}"].ap()[:, tc * DIN:(tc + 1) * DIN],
                    in_=S("mg2")))

        prev = None
        for i, tc in enumerate(order):
            if MAIN_LIMIT is not None and i >= MAIN_LIMIT:
                break
            bi = i % 2
            k.wait("scalar", "dv", xsj_free[bi])
            qx = k.inc("scalar", "qsc", 16, lambda e, tc=tc, bi=bi, d=d: e.dma_start(
                out=xsj[bi], in_=k.dr[f"xstd_{d}"].ap()[:, tc * DIN:(tc + 1) * DIN]))
            k.wait("scalar", "dv", szj_free[bi])
            qz = k.inc("scalar", "qsc", 16, lambda e, tc=tc, bi=bi: e.dma_start(
                out=szj[bi], in_=k.dr["siluzd"].ap()[:, tc * DIN:(tc + 1) * DIN]))
            qy3 = 0
            if d == "r":
                k.wait("scalar", "dv", y3j_free[bi])
                qy3 = k.inc("scalar", "qsc", 16, lambda e, tc=tc, bi=bi: e.dma_start(
                    out=y3j[bi], in_=k.dr["y3fd"].ap()[:, tc * DIN:(tc + 1) * DIN]))
            k.wait("sync", "pe", rowbuf_free)
            qr = k.inc("sync", "qsy", 16, lambda e, tc=tc, d=d: e.dma_start(
                out=S("rowbuf"),
                in_=S(f"stack96_{d}")[0:64, tc * 128:(tc + 1) * 128]))
            k.bank_guard(0)
            k.inc("tensor", "pe", 1, lambda e, tc=tc, d=d: e.matmul(
                k.bank(0, 128), S(f"bcfm_{d}")[:, tc * 128:(tc + 1) * 128],
                S(f"bcfm_{d}")[:, T + tc * 128:T + (tc + 1) * 128],
                start=True, stop=True))
            pem = k.mark("pe")
            k.wait("scalar", "pe", pem)
            k.wait("scalar", "dv", prev["dvm_m2"] if prev is not None
                   else getattr(k, "m_last_m2", 0))
            k.inc("scalar", "ac", 1, lambda e: e.activation(
                S("mscoresT"), k.bank(0, 128), AF.Copy))
            k.bank_consumed(0, "ac")
            if prev is not None:
                emit_block(prev)
            ydm = getattr(k, "mm_prev_ydyo", 0)
            k.wait("scalar", "pe", ydm)
            k.wait("tensor", "qsy", qr)
            for q8 in range(8):
                ab = 1 + (q8 % 2)
                k.bank_guard(ab)
                k.inc("tensor", "pe", 1, lambda e, q8=q8, ab=ab: e.matmul(
                    k.bank(ab, 512), S("ones2"),
                    S("rowbuf")[:, q8 * 512:(q8 + 1) * 512],
                    start=True, stop=False))
                k.inc("tensor", "pe", 1, lambda e, q8=q8, ab=ab, tc=tc, d=d: e.matmul(
                    k.bank(ab, 512).rearrange("p (h l) -> p h l", h=4),
                    S(f"stack96_{d}")[0:96, tc * 128:(tc + 1) * 128],
                    S("ohc")[:, q8 * 4:(q8 + 1) * 4]
                        .unsqueeze(2).broadcast_to([96, 4, 128]),
                    start=False, stop=False))
                k.inc("tensor", "pe", 1, lambda e, q8=q8, ab=ab, nbl=nbl: e.matmul(
                    k.bank(ab, 512), S("idbf"), nbl,
                    start=False, stop=True))
                ab3 = 3 if q8 % 2 == 0 else 0
                k.bank_guard(ab3)
                k.inc("tensor", "pe", 1, lambda e, q8=q8, ab3=ab3: e.matmul(
                    k.bank(ab3, 512), S("ones2"),
                    S("rowbuf")[:, q8 * 512:(q8 + 1) * 512],
                    start=True, stop=True))
                pem = k.mark("pe")
                if q8 == 7:
                    rowbuf_free = pem
                k.wait("scalar", "pe", pem)
                k.inc("scalar", "ac", 1, lambda e, q8=q8, ab=ab: e.activation(
                    S("Ebuf")[:, q8 * 512:(q8 + 1) * 512], k.bank(ab, 512), AF.Exp))
                k.bank_consumed(ab, "ac")
                k.inc("scalar", "ac", 1, lambda e, q8=q8, ab3=ab3: e.activation(
                    S("aEbuf")[:, q8 * 512:(q8 + 1) * 512], k.bank(ab3, 512), AF.Exp))
                k.bank_consumed(ab3, "ac")
                acm = k.mark("ac")
                k.wait("vector", "ac", acm)
                k.inc("vector", "dv", 1, lambda e, q8=q8: e.tensor_tensor(
                    S("Ebuf")[:, q8 * 512:(q8 + 1) * 512]
                        .rearrange("p (h l) -> p h l", h=4),
                    S("Ebuf")[:, q8 * 512:(q8 + 1) * 512]
                        .rearrange("p (h l) -> p h l", h=4),
                    S("mscoresT").unsqueeze(1).broadcast_to([128, 4, 128]),
                    ALU.mult))
                k.wait("gpsimd", "ac", acm)
                k.inc("gpsimd", "pl", 1, lambda e, q8=q8, tc=tc, d=d: e.tensor_tensor(
                    S("aEbuf")[:, q8 * 512:(q8 + 1) * 512]
                        .rearrange("p (h l) -> p h l", h=4),
                    S(f"bcfm_{d}")[:, T + tc * 128:T + (tc + 1) * 128]
                        .unsqueeze(1).broadcast_to([128, 4, 128]),
                    S("aEbuf")[:, q8 * 512:(q8 + 1) * 512]
                        .rearrange("p (h l) -> p h l", h=4),
                    ALU.mult))
                if q8 == 3:
                    k.m2_half_dv = k.mark("dv")
                    k.m2_half_pl = k.mark("pl")
            k.wait("gpsimd", "qsc", qx)
            k.wait("gpsimd", "pe", ydm)
            k.inc("gpsimd", "pl", 1, lambda e, tc=tc, bi=bi, d=d: e.tensor_tensor(
                S("mxds").rearrange("p (h q) -> p h q", h=NH),
                xsj[bi].rearrange("p (h q) -> p h q", h=NH),
                S(f"ddt_{d}")[:, tc * 32:(tc + 1) * 32]
                    .unsqueeze(2).broadcast_to([128, NH, HD]), ALU.mult))
            prev = dict(tc=tc, bi=bi, qx=qx, qz=qz, qy3=qy3,
                        dvm_m2=k.mark("dv"), plm=k.mark("pl"),
                        m2h_dv=k.m2_half_dv, m2h_pl=k.m2_half_pl,
                        pl_xds=k.mark("pl"))
            k.m_last_m2 = prev["dvm_m2"]
            k.m_rowbuf_free = rowbuf_free
            k.m_xsj_free = xsj_free; k.m_szj_free = szj_free
            k.m_y3j_free = y3j_free; k.m_ysb_free = ysb_free
        if prev is not None:
            emit_block(prev)
        k.m_rowbuf_free = rowbuf_free
        k.m_xsj_free = xsj_free; k.m_szj_free = szj_free
        k.m_y3j_free = y3j_free; k.m_ysb_free = ysb_free

    def ph8_taps():
        k.tap("dbc_f", lambda: S("dbc_f"), [128, NH], F32, waits=("dv",))
        k.tap("dbc_r", lambda: S("dbc_r"), [128, NH], F32, waits=("dv",))
        k.tap("Ebuf", lambda: S("Ebuf"), [128, 4096], BF16, waits=("dv", "ac"))
        k.tap("aEbuf", lambda: S("aEbuf"), [128, 4096], BF16, waits=("dv", "ac"))
        k.tap("rowbuf", lambda: S("rowbuf"), [2, 4096], BF16, waits=("qsy",))
        k.tap("mscoresT", lambda: S("mscoresT"), [128, 128], BF16, waits=("ac",))
        k.tap("mg2", lambda: S("mg2"), [128, DIN], BF16, waits=("dv",))

    def ph8():
        k.barrier()
        nq = k.inc("gpsimd", "qgp", 16, lambda e: e.dma_start(
            out=S("nblf"), in_=DI["nblfc"]))
        nq = k.inc("gpsimd", "qgp", 16, lambda e: e.dma_start(
            out=S("nblr"), in_=DI["nblrc"]))
        k.wait("tensor", "qgp", nq)
        for d in MAIN_DIRS:
            if f"ypred_{d}" in DEBUG_TAPS:
                dscratch(f"ypred_{d}", [128, NCHUNK * DIN], BF16)
            main_dir(d)
        ph8_taps()

    if BUILD_UPTO >= 8:
        ph8()
        phases_built.append(8)

    p9, _ = carve(nc, es, TB, [
        ("ysumFM", [128, NCHUNK * DIN], BF16),
        ("ych0", [128, DIN], BF16), ("ych1", [128, DIN], BF16),
    ])
    k.sb.update(p9)

    def ph9():
        k.barrier()
        ych = [S("ych0"), S("ych1")]
        yFM3 = S("ysumFM").rearrange("p (fc t) -> p fc t", fc=16)
        ych_free = [0, 0]
        for tc in range(NCHUNK):
            bi = tc % 2
            k.wait("scalar", "qsy", ych_free[bi])
            qm = k.inc("scalar", "qsc", 16, lambda e, tc=tc, bi=bi: e.dma_start(
                out=ych[bi], in_=k.dr["ysumd"].ap()[:, tc * DIN:(tc + 1) * DIN]))
            k.wait("sync", "qsc", qm)
            ych_free[bi] = k.inc("sync", "qsy", 16, lambda e, tc=tc, bi=bi:
                                 e.dma_start(
                out=yFM3[:, :, tc * 128:(tc + 1) * 128],
                in_=ych[bi].rearrange("p (fc f) -> p fc f", fc=16),
                transpose=True))

    if BUILD_UPTO >= 9:
        ph9()
        phases_built.append(9)

    DEAD = CARVE_OFFS["stack96_f"]
    p10, _ = carve(nc, es, DEAD, [
        ("woutTs", [128, NCHUNK * 1024], BF16),
        ("xfm0", [128, T], F32), ("xfm1", [128, T], F32),
        ("wevac", [128, T], F32),
    ])
    k.sb.update(p10)

    def ph10():
        wq = k.inc("gpsimd", "qgp", 16, lambda e: e.dma_start(
            out=S("woutTs").rearrange("p (fc m) -> p fc m", fc=16),
            in_=DI["woutT"].rearrange("(fc p) m -> p fc m", p=128)))
        k.wait("tensor", "qgp", wq)
        xfm = [S("xfm0"), S("xfm1")]
        xfm_free = [0, 0]
        wevac_hold = 0
        for do in range(8):
            bi = do % 2
            k.wait("scalar", "dv", xfm_free[bi])
            qx = k.inc("scalar", "qsc", 16, lambda e, do=do, bi=bi: e.dma_start(
                out=xfm[bi], in_=DI["xfm"][do * 128:(do + 1) * 128, :]))
            base = (do % 2) * 4
            for q in range(4):
                k.bank_guard(base + q)
            k.wait("tensor", "qsy")
            for fc in range(NCHUNK):
                for tq in range(4):
                    k.inc("tensor", "pe", 1, lambda e, do=do, fc=fc, tq=tq,
                          base=base: e.matmul(
                        k.bank(base + tq, 512),
                        S("woutTs")[:, fc * 1024 + do * 128: fc * 1024 + (do + 1) * 128],
                        S("ysumFM")[:, fc * DIN + tq * 512: fc * DIN + (tq + 1) * 512],
                        start=(fc == 0), stop=(fc == NCHUNK - 1)))
            pem = k.mark("pe")
            k.wait("vector", "pe", pem)
            k.wait("vector", "qsc", qx)
            k.wait("vector", "qgp", wevac_hold)
            for tq in range(4):
                k.inc("vector", "dv", 1, lambda e, tq=tq, bi=bi, base=base:
                      e.scalar_tensor_tensor(
                    S("wevac")[:, tq * 512:(tq + 1) * 512],
                    k.bank(base + tq, 512), 1.0,
                    xfm[bi][:, tq * 512:(tq + 1) * 512], ALU.mult, ALU.add))
                k.bank_consumed(base + tq, "dv")
            xfm_free[bi] = k.cnt["dv"]
            dvm = k.mark("dv")
            k.wait("gpsimd", "dv", dvm)
            wevac_hold = k.inc("gpsimd", "qgp", 16, lambda e, do=do: e.dma_start(
                out=k.dr["accd"].ap()[:, do * T:(do + 1) * T], in_=S("wevac")))

    if BUILD_UPTO >= 10:
        ph10()
        phases_built.append(10)

    p11a, _ = carve(nc, es, TB, [
        ("fuse", [128, 16 * T], BF16),
        ("wbufA0", [128, 8 * 1024], BF16), ("wbufA1", [128, 8 * 1024], BF16),
        ("wT8", [8, NCHUNK * 128], BF16),
    ])
    p11b, _ = carve(nc, es, DEAD, [
        ("dfull", [128, 16 * 1024], BF16),
        ("sgb", [128, T], BF16), ("webc", [128, T], BF16),
        ("fevac0", [128, T], F32), ("fevac1", [128, T], F32),
    ], limit=TB)
    k.sb.update(p11a); k.sb.update(p11b)

    def ph11():
        k.barrier()
        for tc in range(NCHUNK):
            k.bank_guard(0)
            k.inc("tensor", "pe", 1, lambda e, tc=tc: e.transpose(
                k.bank(0, 128)[0:8, :], S("fullw")[:, tc * 8:(tc + 1) * 8],
                S("idf32")))
            pem = k.mark("pe")
            k.wait("scalar", "pe", pem)
            k.inc("scalar", "ac", 1, lambda e, tc=tc: e.activation(
                S("wT8")[:, tc * 128:(tc + 1) * 128], k.bank(0, 128)[0:8, :],
                AF.Copy))
            k.bank_consumed(0, "ac")
        wA = [S("wbufA0"), S("wbufA1")]
        wA_free = [0, 0]
        dq_hold = 0
        fevac_hold2 = [0, 0]
        xnT3 = S("xnT").rearrange("p (kc t) -> p kc t", kc=8)
        for e_ in range(NE):
            for q in range(4):
                k.bank_guard(4 + q)
                k.inc("tensor", "pe", 1, lambda e, e_=e_, q=q: e.matmul(
                    k.bank(4 + q, 512), S("sel8")[:, e_ * 128:(e_ + 1) * 128],
                    S("wT8")[:, q * 512:(q + 1) * 512], start=True, stop=True))
            pem = k.mark("pe")
            k.wait("scalar", "pe", pem)
            k.wait("scalar", "dv")
            for q in range(4):
                k.inc("scalar", "ac", 1, lambda e, q=q: e.activation(
                    S("webc")[:, q * 512:(q + 1) * 512], k.bank(4 + q, 512),
                    AF.Copy))
                k.bank_consumed(4 + q, "ac")
            for Fq in range(4):
                if Fq == 3:
                    k.wait("gpsimd", "pe")
                    dq_hold = k.inc("gpsimd", "qgp", 16,
                                    lambda e, e_=e_: e.dma_start(
                        out=S("dfull").rearrange("p (fc m) -> p fc m", fc=16),
                        in_=DI["dEFD"][e_].rearrange("(fc p) m -> p fc m", p=128)))
                bi = Fq % 2
                k.wait("gpsimd", "pe", wA_free[bi])
                qmf = []
                for fcq in range(4):
                    col = Fq * 512 + fcq * 128
                    for gu, nm in ((0, "gEDF"), (1, "uEDF")):
                        qm = k.inc("gpsimd", "qgp", 16,
                                   lambda e, e_=e_, col=col, bi=bi, gu=gu, nm=nm:
                                   e.dma_start(
                            out=wA[bi][:, gu * 4096:(gu + 1) * 4096]
                                .rearrange("p (kc m) -> p kc m", kc=8)
                                [:, :, (col % 512):(col % 512) + 128],
                            in_=DI[nm][e_][:, col:col + 128]
                                .rearrange("(kc p) m -> p kc m", p=128)))
                    qmf.append(qm)
                for fcq in range(4):
                    k.wait("tensor", "qgp", qmf[fcq])
                    FCg = Fq * 4 + fcq
                    for th in range(2):
                        gi = getattr(k, "ffn_gi", 0)
                        k.ffn_gi = gi + 1
                        base = (gi % 2) * 4
                        for q in range(4):
                            k.bank_guard(base + q)
                        for kc in range(8):
                            for gu in range(2):
                                for ts_ in range(2):
                                    tcol = th * 1024 + ts_ * 512
                                    woff = gu * 4096 + kc * 512 + fcq * 128
                                    k.inc("tensor", "pe", 1, lambda e, kc=kc,
                                          ts_=ts_, gu=gu, bi=bi, base=base,
                                          tcol=tcol, woff=woff: e.matmul(
                                        k.bank(base + gu * 2 + ts_, 512),
                                        wA[bi][:, woff:woff + 128],
                                        xnT3[:, kc, tcol:tcol + 512],
                                        start=(kc == 0), stop=(kc == 7)))
                        pem = k.mark("pe")
                        if fcq == 3 and th == 1:
                            wA_free[bi] = pem
                        k.wait("scalar", "pe", pem)
                        k.wait("scalar", "dv")
                        k.inc("scalar", "ac", 1, lambda e, th=th, base=base:
                              e.activation(
                            S("sgb")[:, th * 1024:(th + 1) * 1024],
                            k.ps.ap()[:, base * 512: base * 512 + 1024], AF.Silu))
                        acm = k.mark("ac")
                        k.bank_consumed(base, "ac"); k.bank_consumed(base + 1, "ac")
                        k.wait("vector", "ac", acm)
                        k.inc("vector", "dv", 1, lambda e, th=th: e.tensor_tensor(
                            S("sgb")[:, th * 1024:(th + 1) * 1024],
                            S("sgb")[:, th * 1024:(th + 1) * 1024],
                            S("webc")[:, th * 1024:(th + 1) * 1024], ALU.mult))
                        k.inc("vector", "dv", 1, lambda e, FCg=FCg, th=th,
                              base=base: e.scalar_tensor_tensor(
                            S("fuse")[:, FCg * T + th * 1024: FCg * T + (th + 1) * 1024],
                            k.ps.ap()[:, (base + 2) * 512: (base + 2) * 512 + 1024],
                            1.0, S("sgb")[:, th * 1024:(th + 1) * 1024],
                            ALU.mult, ALU.mult))
                        k.bank_consumed(base + 2, "dv"); k.bank_consumed(base + 3, "dv")
            k.wait("tensor", "qgp", dq_hold)
            k.wait("tensor", "dv")
            for do in range(8):
                base = (do % 2) * 4
                for q in range(4):
                    k.bank_guard(base + q)
                for fc in range(NCHUNK):
                    for tq in range(4):
                        k.inc("tensor", "pe", 1, lambda e, do=do, fc=fc, tq=tq,
                              base=base: e.matmul(
                            k.bank(base + tq, 512),
                            S("dfull")[:, fc * 1024 + do * 128:
                                       fc * 1024 + (do + 1) * 128],
                            S("fuse")[:, fc * T + tq * 512: fc * T + (tq + 1) * 512],
                            start=(fc == 0), stop=(fc == NCHUNK - 1)))
                pem = k.mark("pe")
                fb = do % 2
                fev = S("fevac0") if fb == 0 else S("fevac1")
                k.wait("vector", "pe", pem)
                k.wait("vector", "qgp", fevac_hold2[fb])
                for tq in range(4):
                    k.inc("vector", "dv", 1, lambda e, tq=tq, base=base, fev=fev:
                          e.tensor_copy(
                        fev[:, tq * 512:(tq + 1) * 512],
                        k.bank(base + tq, 512)))
                    k.bank_consumed(base + tq, "dv")
                dvm = k.mark("dv")
                k.wait("gpsimd", "dv", dvm)
                fevac_hold2[fb] = k.inc("gpsimd", "qgp", 16, lambda e, do=do,
                                        fev=fev: e.dma_start(
                    out=k.dr["accd"].ap()[:, do * T:(do + 1) * T],
                    in_=fev, accum_op=ALU.add))

    if BUILD_UPTO >= 11:
        ph11()
        phases_built.append(11)

    def ph12():
        k.wait("gpsimd", "qgp")
        k.wait("scalar", "qgp")
        k.inc("gpsimd", "qgp", 16, lambda e: e.dma_start(
            out=k.dr["ofm"].ap()[:, 0:4 * T], in_=k.dr["accd"].ap()[:, 0:4 * T]))
        k.inc("scalar", "qsc", 16, lambda e: e.dma_start(
            out=k.dr["ofm"].ap()[:, 4 * T:8 * T],
            in_=k.dr["accd"].ap()[:, 4 * T:8 * T]))
        k.wait("gpsimd", "qgp")
        k.wait("gpsimd", "qsc")

    if BUILD_UPTO >= 12:
        ph12()
        phases_built.append(12)

    if k.tap0 is not None:
        k.wait("scalar", "ac")
        k.wait("scalar", "dv")
        k.inc("scalar", "qsc", 16, lambda e: e.dma_start(
            out=k.tap0.ap()[:, 0:1], in_=k.c0ap, allow_slow_non_contiguous=True))
        k.inc("scalar", "qsc", 16, lambda e: e.dma_start(
            out=k.tap0.ap()[:, 1:2], in_=k.c1ap, allow_slow_non_contiguous=True))

    return nc, es, k, DI, phases_built


def finalize(nc, es, k):
    k.wait("sync", "qsc")
    k.wait("sync", "qsy")
    k.wait("sync", "qgp")
    k.wait("sync", "cc")
    k.wait("sync", "pe")
    k.wait("sync", "dv")
    k.wait("sync", "ac")
    k.wait("sync", "pl")
    with nc.Block() as block:
        for eng in Bld.ENGS:
            fns = k.ops[eng]
            if not fns:
                continue
            getattr(block, eng)(lambda e, fns=fns: [fn(e) for fn in fns])
    es.close()
    return nc


def get_program():
    key = (BUILD_UPTO, frozenset(DEBUG_TAPS))
    if key not in _cache:
        nc, es, k, DI, ph = build_program()
        finalize(nc, es, k)
        _cache[key] = nc
    return _cache[key]


def _host_consts():
    f = np.float32
    idx = np.arange(128)
    UL = (idx[:, None] <= idx[None, :]).astype(f)
    UR = (idx[:, None] >= idx[None, :]).astype(f)
    sel127 = np.zeros((128, 128), f); sel127[127, :] = 1.0
    sel0 = np.zeros((128, 128), f); sel0[0, :] = 1.0
    idf = np.eye(128, dtype=f)
    ones2 = np.ones((2, 128), f)
    ohc = np.concatenate([-np.eye(NH, dtype=f), -np.eye(NH, dtype=f),
                          np.eye(NH, dtype=f)], axis=0)
    nblf1 = np.where(idx[None, :] < idx[:, None], -f(BIG), f(0.0))
    nblr1 = np.where(idx[None, :] > idx[:, None], -f(BIG), f(0.0))
    nblf = np.tile(nblf1[:, None, :], (1, 4, 1)).reshape(128, 512)
    nblr = np.tile(nblr1[:, None, :], (1, 4, 1)).reshape(128, 512)
    ones1f = np.ones((1, 128), f)
    sel8 = np.zeros((NE, 8 * 128), f)
    for e_ in range(NE):
        sel8[e_, e_ * 128:(e_ + 1) * 128] = 1.0
    return dict(ulc=UL, urc=UR, sel127c=sel127, sel0c=sel0, idf32c=idf,
                idbfc=idf, ones2c=ones2, ohcc=ohc, nblfc=nblf.astype(f),
                nblrc=nblr.astype(f), ones1fc=ones1f, sel8c=sel8)


def _stage(inputs):
    gi = lambda n: np.asarray(inputs[n])
    x = gi("L0_attn_x")
    xT = np.ascontiguousarray(x.transpose(0, 2, 1))
    winT = np.ascontiguousarray(gi("L0_attn_W_in").T)
    woutT = np.ascontiguousarray(gi("L0_attn_W_out").T)
    consts = _host_consts()
    shared = dict(winT=winT, woutT=woutT,
                  router=np.ascontiguousarray(gi("L0_ffn_router_kernel_DE")),
                  gEDF=np.ascontiguousarray(gi("L0_ffn_kernel_gating_EDF")),
                  uEDF=np.ascontiguousarray(gi("L0_ffn_kernel_up_EDF")),
                  dEFD=np.ascontiguousarray(gi("L0_ffn_kernel_down_EFD")),
                  **consts)
    for d in ("f", "r"):
        shared[f"convw_{d}"] = np.ascontiguousarray(gi(f"L0_attn_conv_w_{d}"))
        shared[f"convb_{d}"] = np.ascontiguousarray(
            gi(f"L0_attn_conv_b_{d}")[:, None])
        shared[f"dtb_{d}"] = np.ascontiguousarray(gi(f"L0_attn_dt_bias_{d}")[:, None])
        shared[f"alog_{d}"] = np.ascontiguousarray(gi(f"L0_attn_A_log_{d}")[:, None])
        shared[f"drow_{d}"] = np.ascontiguousarray(gi(f"L0_attn_D_{d}")[None, :])
        shared[f"nwrow_{d}"] = np.ascontiguousarray(gi(f"L0_attn_norm_w_{d}")[None, :])
    in_maps = []
    for c in range(8):
        b, half = c // 2, c % 2
        fl = np.zeros(8, np.float32)
        if half == 0:
            fl[[0, 3, 4, 7]] = 1.0
        else:
            fl[[1, 2, 5, 6]] = 1.0
        m = dict(shared)
        m["xtok"] = np.ascontiguousarray(x[b, half * T:(half + 1) * T, :])
        m["xfm"] = np.ascontiguousarray(xT[b][:, half * T:(half + 1) * T])
        m["flagsc"] = np.tile(fl[None, :], (128, 1)).copy()
        in_maps.append(m)
    return in_maps


def run_cores(inputs):
    nc = get_program()
    in_maps = _stage(inputs)
    res = run_bass_kernel_spmd(nc, in_maps, core_ids=list(range(8)))
    return res


def kernel(**inputs):
    res = run_cores(inputs)
    out = np.empty((4, 4096, D), np.float32)
    for c in range(8):
        b, half = c // 2, c % 2
        ofm = np.asarray(res.results[c]["ofm"], dtype=np.float32).reshape(128, 8, T)
        out[b, half * T:(half + 1) * T, :] = (
            ofm.transpose(2, 1, 0).reshape(T, D))
    return out


# version 28
# speedup vs baseline: 1.0449x; 1.0449x over previous
import math
import numpy as np
import ml_dtypes
from contextlib import ExitStack

import concourse.bass as bass
import concourse.mybir as mybir
from concourse.bass_utils import run_bass_kernel_spmd

F32 = mybir.dt.float32
BF16 = mybir.dt.bfloat16
AF = mybir.ActivationFunctionType
ALU = mybir.AluOpType

D = 1024
T = 2048
DIN = 2048
NH = 32
HD = 64
DST = 128
CONVD = 2304
DIP = 4384
NCHUNK = 16
CQ = 128
NE = 8
FF = 2048
EPS = 1e-5
BIG = 1e9
TPAD = 2056

KB = 1024

DEBUG_TAPS = set()
BUILD_UPTO = 99
SKIP_ROUTER = False
MAIN_LIMIT = None
MAIN_DIRS = ("f", "r")

_cache = {}


class Bld:
    ENGS = ("tensor", "vector", "scalar", "gpsimd", "sync")

    def __init__(self, nc, es):
        self.nc = nc
        self.es = es
        self.ops = {e: [] for e in self.ENGS}
        self.sems = {}
        self.cnt = {}
        self.sb = {}
        self.dr = {}
        self.ps = None

    def sem(self, name):
        s = self.es.enter_context(self.nc.semaphore(name))
        self.sems[name] = s
        self.cnt[name] = 0
        return s

    def emit(self, eng, fn):
        self.ops[eng].append(fn)

    def inc(self, eng, tag, n, fn):
        self.cnt[tag] += n
        sem = self.sems[tag]

        def wrapped(e, fn=fn, sem=sem, n=n):
            fn(e).then_inc(sem, n)
        self.emit(eng, wrapped)
        return self.cnt[tag]

    def wait(self, eng, tag, at=None):
        v = self.cnt[tag] if at is None else at
        if v <= 0:
            return
        sem = self.sems[tag]
        self.emit(eng, lambda e, sem=sem, v=v: e.wait_ge(sem, v))

    def mark(self, tag):
        return self.cnt[tag]

    def barrier(self):
        marks = {t: self.cnt[t] for t in self.cnt}
        for eng in self.ENGS:
            for t, v in marks.items():
                if v > 0:
                    self.wait(eng, t, v)

    def tiny(self, eng, tag, fn):
        self.inc(eng, tag, 1, fn)
        self.wait(eng, tag)

    def bank(self, b, cols=512, off=0, dt=F32):
        ap = self.ps.ap()
        if dt == F32:
            return ap[:, b * 512 + off: b * 512 + off + cols]
        return ap.bitcast(BF16)[:, b * 1024 + off: b * 1024 + off + cols]


CARVE_OFFS = {}


def carve(nc, es, base, specs, limit=16384 + 201728 + 128):
    out = {}
    off = base
    for name, shape, dt in specs:
        esz = 4 if dt == F32 else (2 if dt == BF16 else 1)
        nbytes = int(np.prod(shape[1:])) * esz if len(shape) > 1 else esz
        off = (off + 31) // 32 * 32
        h = nc.alloc_sbuf_tensor_at(name, list(shape), dt, offset=off)
        out[name] = h
        CARVE_OFFS[name] = off
        off += nbytes
    assert off <= limit, f"carve overflow: end={off} > {limit}"
    return out, off


def bcast_ap(ap2d, reps, inner):
    return ap2d.unsqueeze(2).broadcast_to([ap2d.shape[0], ap2d.shape[1], reps])


def bcast_outer(ap2d, reps):
    return ap2d.unsqueeze(1).broadcast_to([ap2d.shape[0], reps, ap2d.shape[1]])


MB = 16896
MSPAN = 201728


def build_program():
    nc = bass.Bass("TRN2", target_bir_lowering=False)
    nc.alloc_sbuf_tensor("manual_guard", [128, MSPAN], mybir.dt.uint8)
    es = ExitStack()
    k = Bld(nc, es)

    DI = {}

    def din(name, shape, dt=F32):
        DI[name] = nc.dram_tensor(name, list(shape), dt, kind="ExternalInput").ap()
        return DI[name]

    din("xtok", [T, D]); din("xfm", [D, T])
    din("winT", [D, DIP]); din("woutT", [DIN, D])
    for d in ("f", "r"):
        din(f"convw_{d}", [CONVD, 4]); din(f"convb_{d}", [CONVD, 1])
        din(f"dtb_{d}", [NH, 1]); din(f"alog_{d}", [NH, 1])
        din(f"drow_{d}", [1, NH]); din(f"nwrow_{d}", [1, DIN])
    din("router", [D, NE])
    din("gEDF", [NE, D, FF]); din("uEDF", [NE, D, FF]); din("dEFD", [NE, FF, D])
    din("ulc", [128, 128]); din("urc", [128, 128])
    din("sel127c", [128, 128]); din("sel0c", [128, 128])
    din("idf32c", [128, 128]); din("idbfc", [128, 128])
    din("ones2c", [2, 128]); din("ohcc", [96, NH])
    din("nblfc", [128, 512]); din("nblrc", [128, 512])
    din("ones1fc", [1, 128]); din("sel8c", [NE, 1024])
    din("flagsc", [128, 8])

    def dscratch(name, shape, dt):
        kind = "ExternalOutput" if name in DEBUG_TAPS else "Internal"
        h = nc.dram_tensor(name, list(shape), dt, kind=kind)
        k.dr[name] = h
        return h

    dscratch("xstd_f", [128, NCHUNK * DIN], BF16)
    dscratch("xstd_r", [128, NCHUNK * DIN], BF16)
    dscratch("siluzd", [128, NCHUNK * DIN], BF16)
    dscratch("y3fd", [128, NCHUNK * DIN], BF16)
    dscratch("ysumd", [128, NCHUNK * DIN], BF16)
    dscratch("accd", [128, 8 * T], F32)
    dscratch("sndE", [128, 128], BF16)
    dscratch("rcvE", [128, 128], BF16)
    dscratch("sndS", [128, 2 * DIN], BF16)
    dscratch("rcvS", [128, 2 * DIN], BF16)
    ofm = nc.dram_tensor("ofm", [128, 8 * T], F32, kind="ExternalOutput")
    k.dr["ofm"] = ofm

    pa, pa_end = carve(nc, es, MB, [
        ("xnT", [128, 8 * T], BF16),
        ("fullw", [128, 128], F32),
        ("ulc", [128, 128], F32), ("urc", [128, 128], F32),
        ("sel127", [128, 128], F32), ("sel0", [128, 128], F32),
        ("idf32", [128, 128], F32), ("idbf", [128, 128], BF16),
        ("ones2", [2, 128], BF16), ("ohc", [96, NH], BF16),
        ("ones1f", [1, 128], F32), ("sel8", [NE, 1024], BF16),
        ("flags", [128, 8], F32),
        ("convw_f", [128, 72], F32), ("convw_r", [128, 72], F32),
        ("convb_f", [128, 18], F32), ("convb_r", [128, 18], F32),
        ("dtbcol_f", [NH, 1], F32), ("dtbcol_r", [NH, 1], F32),
        ("nacol_f", [NH, 1], F32), ("nacol_r", [NH, 1], F32),
        ("alogcol_f", [NH, 1], F32), ("alogcol_r", [NH, 1], F32),
    ])
    pb, pb_end = carve(nc, es, pa_end, [
        ("stack96_f", [96, NCHUNK * 128], BF16), ("stack96_r", [96, NCHUNK * 128], BF16),
        ("lbce_f", [128, NCHUNK * NH], BF16), ("lbce_r", [128, NCHUNK * NH], BF16),
        ("ddt_f", [128, NCHUNK * NH], BF16), ("ddt_r", [128, NCHUNK * NH], BF16),
        ("bcfm_f", [128, 2 * T], BF16), ("bcfm_r", [128, 2 * T], BF16),
        ("btok_f", [128, T], BF16), ("btok_r", [128, T], BF16),
        ("s0t", [128, DIN], BF16),
        ("sin_f", [128, DIN], BF16), ("sin_r", [128, DIN], BF16),
        ("nwbc_f", [128, DIN], BF16), ("nwbc_r", [128, DIN], BF16),
        ("dbc_f", [128, NH], F32), ("dbc_r", [128, NH], F32),
        ("rsb", [128, 8 * NE], F32),
        ("rhi", [128, NE * NE], BF16), ("rlo", [128, NE * NE], BF16),
    ])
    TB = pb_end
    k.sb.update(pa); k.sb.update(pb)
    k.TB = TB
    assert TB <= MB + 102 * KB, f"persist too large: {TB}"
    p0, _ = carve(nc, es, TB, [
        ("drow_f", [1, NH], F32), ("drow_r", [1, NH], F32),
        ("nwrow_f", [1, DIN], BF16), ("nwrow_r", [1, DIN], BF16),
    ])
    k.sb.update(p0)

    k.ps = es.enter_context(nc.psum_tensor("PS", [128, 4096], F32))

    for s in ("qgp", "qsy", "qsc", "pe", "dv", "ac", "pl", "cc"):
        k.sem(s)

    sb = k.sb

    def S(name):
        return k.sb[name].ap()

    k.S = S
    k.DI = DI

    phases_built = []

    def ph0():
        g = lambda fn: k.inc("gpsimd", "qgp", 16, fn)
        for nm in ("ulc", "urc", "sel127", "sel0", "idf32", "ones1f", "sel8",
                   "flags", "fullw"):
            pass
        g(lambda e: e.dma_start(out=S("ulc"), in_=DI["ulc"]))
        g(lambda e: e.dma_start(out=S("urc"), in_=DI["urc"]))
        g(lambda e: e.dma_start(out=S("sel127"), in_=DI["sel127c"]))
        g(lambda e: e.dma_start(out=S("sel0"), in_=DI["sel0c"]))
        g(lambda e: e.dma_start(out=S("idf32"), in_=DI["idf32c"]))
        g(lambda e: e.dma_start(out=S("ones1f"), in_=DI["ones1fc"]))
        g(lambda e: e.dma_start(out=S("sel8"), in_=DI["sel8c"]))
        g(lambda e: e.dma_start(out=S("flags"), in_=DI["flagsc"]))
        g(lambda e: e.dma_start(out=S("idbf"), in_=DI["idbfc"]))
        g(lambda e: e.dma_start(out=S("ones2"), in_=DI["ones2c"]))
        g(lambda e: e.dma_start(out=S("ohc"), in_=DI["ohcc"]))
        for d in ("f", "r"):
            g(lambda e, d=d: e.dma_start(
                out=S(f"convw_{d}").rearrange("p (j c) -> p j c", c=4),
                in_=DI[f"convw_{d}"].rearrange("(j p) c -> p j c", p=128)))
            g(lambda e, d=d: e.dma_start(
                out=S(f"convb_{d}").unsqueeze(2),
                in_=DI[f"convb_{d}"].rearrange("(j p) c -> p j c", p=128),
                allow_slow_non_contiguous=True))
            g(lambda e, d=d: e.dma_start(out=S(f"dtbcol_{d}"), in_=DI[f"dtb_{d}"]))
            g(lambda e, d=d: e.dma_start(out=S(f"alogcol_{d}"), in_=DI[f"alog_{d}"]))
            g(lambda e, d=d: e.dma_start(out=S(f"drow_{d}"), in_=DI[f"drow_{d}"]))
            g(lambda e, d=d: e.dma_start(out=S(f"nwrow_{d}"), in_=DI[f"nwrow_{d}"]))
        g(lambda e: e.dma_start(
            out=S("rsb").rearrange("p (kc e) -> p kc e", e=NE),
            in_=DI["router"].rearrange("(kc p) e -> p kc e", p=128)))
        ld0 = k.mark("qgp")

        k.wait("scalar", "qgp", ld0)
        k.inc("scalar", "ac", 1, lambda e: e.activation(
            S("nacol_f"), S("alogcol_f"), AF.Exp, scale=1.0))
        k.inc("scalar", "ac", 1, lambda e: e.activation(
            S("nacol_r"), S("alogcol_r"), AF.Exp, scale=1.0))
        k.wait("vector", "ac")
        k.inc("vector", "dv", 1, lambda e: e.tensor_scalar(
            S("nacol_f"), S("nacol_f"), -1.0, None, ALU.mult))
        k.inc("vector", "dv", 1, lambda e: e.tensor_scalar(
            S("nacol_r"), S("nacol_r"), -1.0, None, ALU.mult))
        k.wait("vector", "qgp", ld0)
        rsb3 = S("rsb").rearrange("p (kc e) -> p kc e", e=NE)
        rhi3 = S("rhi").rearrange("p (kc e) -> p kc e", e=NE)
        rlo3 = S("rlo").rearrange("p (kc e) -> p kc e", e=NE)
        k.tiny("vector", "dv", lambda e: e.tensor_copy(rhi3, rsb3))
        k.inc("vector", "dv", 1, lambda e: e.scalar_tensor_tensor(
            rlo3, rsb3, 1.0, rhi3, ALU.mult, ALU.subtract))
        k.rhi_dv = k.cnt["dv"]
        k.wait("tensor", "qgp", ld0)
        for d in ("f", "r"):
            for q in range(4):
                k.inc("tensor", "pe", 1, lambda e, d=d, q=q: e.matmul(
                    k.bank(q, 512), S("ones2")[0:1, :],
                    S(f"nwrow_{d}")[:, q * 512:(q + 1) * 512],
                    start=True, stop=True))
            k.wait("scalar", "pe")
            k.inc("scalar", "ac", 1, lambda e, d=d: e.activation(
                S(f"nwbc_{d}"), k.ps.ap()[:, 0:2048], AF.Copy))
            k.wait("tensor", "ac")
        for d in ("f", "r"):
            k.inc("tensor", "pe", 1, lambda e, d=d: e.matmul(
                k.bank(4, NH), S("ones1f"), S(f"drow_{d}"), start=True, stop=True))
            k.wait("vector", "pe")
            k.inc("vector", "dv", 1, lambda e, d=d: e.tensor_copy(
                S(f"dbc_{d}"), k.bank(4, NH)))
            k.wait("tensor", "dv")
        k.ld0 = ld0

    def tap(name, ap_fn, shape, dt, waits=()):
        if name not in DEBUG_TAPS:
            return
        h = nc.dram_tensor(name + "__dbg", list(shape), dt, kind="ExternalOutput")
        for tag in waits:
            k.wait("scalar", tag)
        k.inc("scalar", "qsc", 16, lambda e: e.dma_start(out=h.ap(), in_=ap_fn()))
    k.tap = tap

    bank_last = {}

    def bank_guard(b):
        if b in bank_last:
            tag, cnt = bank_last[b]
            k.wait("tensor", tag, cnt)

    def bank_consumed(b, tag):
        bank_last[b] = (tag, k.cnt[tag])
    k.bank_guard = bank_guard
    k.bank_consumed = bank_consumed

    k.c0ap = nc.const_aps.tensor(0.0, [128, 1], F32)
    k.c1ap = nc.const_aps.tensor(1.0, [128, 1], F32)
    k.tap0 = None
    if "constap" in DEBUG_TAPS:
        h0 = nc.dram_tensor("constap__dbg", [128, 2], F32, kind="ExternalOutput")
        k.tap0 = h0

    ph0()
    phases_built.append(0)

    p1, _ = carve(nc, es, TB, [
        ("xt0", [128, D], F32), ("xt1", [128, D], F32),
        ("xnf", [128, D], F32),
        ("xnb", [128, D], BF16), ("xlo", [128, D], BF16), ("junkb", [128, D], BF16),
        ("junkf", [128, D], F32),
        ("lncol", [128, 128], F32), ("lncolB", [128, 64], F32), ("rtB", [128, 4], F32),
        ("rt", [128, 12], F32),
        ("rte", [128, NE], F32), ("rtp", [128, NE], F32), ("rtq", [128, NE], F32),
        ("rteq1", [128, NE], F32), ("rteq2", [128, NE], F32),
        ("xnloT", [128, 8 * T], BF16),
    ])
    k.sb.update(p1)

    def ph1():
        k.barrier()
        xt = [S("xt0"), S("xt1")]
        xnT3 = S("xnT").rearrange("p (kc t) -> p kc t", kc=8)
        xloT3 = S("xnloT").rearrange("p (kc t) -> p kc t", kc=8)
        buf_free = [0, 0]
        for tc in range(NCHUNK):
            lnb = tc * 8
            bi = tc % 2
            k.wait("scalar", "dv", buf_free[bi])
            qm = k.inc("scalar", "qsc", 16, lambda e, tc=tc, bi=bi: e.dma_start(
                out=xt[bi], in_=DI["xtok"][tc * 128:(tc + 1) * 128, :]))
            k.wait("vector", "qsc", qm)
            k.tiny("vector", "dv", lambda e, bi=bi, lnb=lnb: e.reduce_sum(
                S("lncol")[:, lnb + 0:lnb + 1], xt[bi], axis=mybir.AxisListType.X))
            k.inc("vector", "dv", 1, lambda e, bi=bi: e.tensor_tensor(
                S("junkf"), xt[bi], xt[bi], ALU.mult))
            k.tiny("vector", "dv", lambda e, lnb=lnb: e.reduce_sum(
                S("lncol")[:, lnb + 2:lnb + 3], S("junkf"), axis=mybir.AxisListType.X))
            k.tiny("vector", "dv", lambda e, lnb=lnb: e.tensor_scalar(
                S("lncol")[:, lnb + 1:lnb + 2], S("lncol")[:, lnb + 0:lnb + 1], 1.0 / D, None, ALU.mult))
            k.tiny("vector", "dv", lambda e, lnb=lnb: e.tensor_scalar(
                S("lncol")[:, lnb + 3:lnb + 4], S("lncol")[:, lnb + 1:lnb + 2],
                S("lncol")[:, lnb + 1:lnb + 2], None, ALU.mult))
            k.tiny("vector", "dv", lambda e, lnb=lnb: e.scalar_tensor_tensor(
                S("lncol")[:, lnb + 4:lnb + 5], S("lncol")[:, lnb + 2:lnb + 3], 1.0 / D, S("lncol")[:, lnb + 3:lnb + 4],
                ALU.mult, ALU.subtract))
            k.wait("scalar", "dv")
            lbB = tc * 4
            k.tiny("vector", "dv", lambda e, lnb=lnb: e.tensor_scalar(
                S("lncol")[:, lnb + 5:lnb + 6], S("lncol")[:, lnb + 4:lnb + 5],
                EPS, None, ALU.add))
            k.wait("scalar", "dv")
            k.tiny("scalar", "ac", lambda e, lnb=lnb, lbB=lbB: e.activation(
                S("lncolB")[:, lbB + 0:lbB + 1], S("lncol")[:, lnb + 5:lnb + 6],
                AF.Sqrt))
            k.wait("vector", "ac")
            k.tiny("vector", "dv", lambda e, lnb=lnb, lbB=lbB: e.reciprocal(
                S("lncol")[:, lnb + 6:lnb + 7], S("lncolB")[:, lbB + 0:lbB + 1]))
            k.tiny("vector", "dv", lambda e, lnb=lnb: e.tensor_scalar(
                S("lncol")[:, lnb + 7:lnb + 8], S("lncol")[:, lnb + 1:lnb + 2],
                S("lncol")[:, lnb + 6:lnb + 7], None, ALU.mult))
            k.inc("vector", "dv", 1, lambda e, bi=bi, lnb=lnb: e.tensor_scalar(
                S("xnf"), xt[bi], S("lncol")[:, lnb + 6:lnb + 7], S("lncol")[:, lnb + 7:lnb + 8],
                ALU.mult, ALU.subtract))
            buf_free[bi] = k.cnt["dv"]
            k.inc("vector", "dv", 1, lambda e: e.tensor_copy(S("xnb"), S("xnf")))
            k.inc("vector", "dv", 1, lambda e: e.scalar_tensor_tensor(
                S("xlo"), S("xnf"), 1.0, S("xnb"), ALU.mult, ALU.subtract))
            dvm = k.mark("dv")
            k.wait("sync", "dv", dvm)
            k.inc("sync", "qsy", 16, lambda e, tc=tc: e.dma_start(
                out=xnT3[:, :, tc * 128:(tc + 1) * 128],
                in_=S("xnb").rearrange("p (kc d) -> p kc d", kc=8), transpose=True))
            qym = k.inc("sync", "qsy", 16, lambda e, tc=tc: e.dma_start(
                out=xloT3[:, :, tc * 128:(tc + 1) * 128],
                in_=S("xlo").rearrange("p (kc d) -> p kc d", kc=8), transpose=True))
            if not hasattr(k, "ph1_t_marks"):
                k.ph1_t_marks = []
            k.ph1_t_marks.append(qym)
            k.wait("vector", "qsy", qym)

            if SKIP_ROUTER:
                continue
            k.wait("tensor", "qsy", qym)
            if tc == 0:
                k.wait("tensor", "dv", k.rhi_dv)
            k.bank_guard(7)
            nmm = 0
            for (lh, rh) in ((S("xnT"), S("rhi")), (S("xnT"), S("rlo")),
                             (S("xnloT"), S("rhi"))):
                for kc in range(8):
                    first = nmm == 0
                    last = nmm == 23
                    k.inc("tensor", "pe", 1, lambda e, lh=lh, rh=rh, kc=kc, tc=tc,
                          first=first, last=last: e.matmul(
                        k.bank(7, NE),
                        lh.rearrange("p (kc t) -> p kc t", kc=8)[:, kc, tc * 128:(tc + 1) * 128],
                        rh[:, kc * NE:(kc + 1) * NE],
                        start=first, stop=last))
                    nmm += 1
            pem = k.mark("pe")
            k.wait("vector", "pe", pem)
            k.tiny("vector", "dv", lambda e: e.reduce_max(
                S("rt")[:, 0:1], k.bank(7, NE), axis=mybir.AxisListType.X))
            k.tiny("vector", "dv", lambda e: e.tensor_scalar(
                S("rt")[:, 1:2], S("rt")[:, 0:1], -1.0, None, ALU.mult))
            k.wait("scalar", "dv")
            k.tiny("scalar", "ac", lambda e: e.activation(
                S("rte"), k.bank(7, NE), AF.Exp, bias=S("rt")[:, 1:2]))
            acm_r = k.mark("ac")
            k.bank_consumed(7, "ac")
            k.wait("vector", "ac", acm_r)
            k.tiny("vector", "dv", lambda e: e.reduce_sum(
                S("rt")[:, 2:3], S("rte"), axis=mybir.AxisListType.X))
            k.tiny("vector", "dv", lambda e: e.reciprocal(
                S("rt")[:, 3:4], S("rt")[:, 2:3]))
            k.tiny("vector", "dv", lambda e: e.tensor_scalar(
                S("rtp"), S("rte"), S("rt")[:, 3:4], None, ALU.mult))
            k.tiny("vector", "dv", lambda e: e.reduce_max(
                S("rt")[:, 4:5], S("rtp"), axis=mybir.AxisListType.X))
            k.tiny("vector", "dv", lambda e: e.tensor_scalar(
                S("rteq1"), S("rtp"), S("rt")[:, 4:5], None, ALU.is_ge))
            k.tiny("vector", "dv", lambda e: e.scalar_tensor_tensor(
                S("rtq"), S("rteq1"), -BIG, S("rtp"), ALU.mult, ALU.add))
            k.tiny("vector", "dv", lambda e: e.reduce_max(
                S("rt")[:, 5:6], S("rtq"), axis=mybir.AxisListType.X))
            k.tiny("vector", "dv", lambda e: e.tensor_scalar(
                S("rteq2"), S("rtq"), S("rt")[:, 5:6], None, ALU.is_ge))
            k.tiny("vector", "dv", lambda e: e.tensor_scalar(
                S("rt")[:, 8:9], S("rt")[:, 4:5], S("rt")[:, 5:6], None, ALU.subtract))
            k.wait("scalar", "dv")
            k.tiny("scalar", "ac", lambda e: e.activation(
                S("rtB")[:, 0:1], S("rt")[:, 8:9], AF.Sigmoid))
            k.wait("vector", "ac")
            k.tiny("vector", "dv", lambda e: e.tensor_scalar(
                S("rt")[:, 7:8], S("rtB")[:, 0:1], -1.0, 1.0, ALU.mult, ALU.add))
            k.tiny("vector", "dv", lambda e: e.tensor_scalar(
                S("rteq1"), S("rteq1"), S("rtB")[:, 0:1], None, ALU.mult))
            k.inc("vector", "dv", 1, lambda e, tc=tc: e.scalar_tensor_tensor(
                S("fullw")[:, tc * NE:(tc + 1) * NE], S("rteq2"), S("rt")[:, 7:8],
                S("rteq1"), ALU.mult, ALU.add))
        k.tap("xnT", lambda: S("xnT"), [128, 8 * T], BF16, waits=("qsy",))
        k.tap("xnloT", lambda: S("xnloT"), [128, 8 * T], BF16, waits=("qsy",))
        k.tap("fullw", lambda: S("fullw"), [128, 128], F32, waits=("dv",))
        k.tap("lncol", lambda: S("lncol"), [128, 128], F32, waits=("dv", "ac"))
        k.tap("lncolB", lambda: S("lncolB"), [128, 64], F32, waits=("dv", "ac"))

    if BUILD_UPTO >= 1:
        ph1()
        phases_built.append(1)

    p2, p2_end = carve(nc, es, TB, [
        ("xBC", [128, 18 * TPAD], BF16),
        ("dtaln_f", [64, T], F32), ("dtaln_r", [64, T], F32),
        ("dt_pre", [NH, T], F32),
        ("wdc0", [128, 1024], BF16), ("wdc1", [128, 1024], BF16),
    ])
    k.sb.update(p2)
    k.conv_scratch_base = p2_end

    def ph2():
        xBC3 = S("xBC").rearrange("p (j c) -> p j c", c=TPAD)
        wdc = [S("wdc0"), S("wdc1")]
        wq_ready = {}

        def emit_wload(dc, wait_pe):
            bi = dc % 2
            cols = 128 if dc < 18 else NH
            k.wait("gpsimd", "pe", wait_pe)
            wq_ready[dc] = k.inc("gpsimd", "qgp", 16,
                lambda e, dc=dc, bi=bi, cols=cols: e.dma_start(
                    out=wdc[bi][:, 0:8 * cols].rearrange("p (kc m) -> p kc m", kc=8),
                    in_=DI["winT"][:, DIN + dc * 128: DIN + dc * 128 + cols]
                        .rearrange("(kc p) m -> p kc m", p=128)))

        emit_wload(0, 0)
        emit_wload(1, 0)
        ldq = k.mark("qsy")
        tmarks = getattr(k, "ph1_t_marks", None)
        k.wait("tensor", "qsy", ldq)
        for dc in range(19):
            bi = dc % 2
            cols = 128 if dc < 18 else NH
            k.wait("tensor", "qgp", wq_ready[dc])
            for b in range(4):
                k.bank_guard(b)
            for th2 in range(2):
                if tmarks is not None:
                    k.wait("tensor", "qsy", tmarks[8 * (th2 + 1) - 1])
                for kc in range(8):
                    for tsl in (2 * th2, 2 * th2 + 1):
                        k.inc("tensor", "pe", 1, lambda e, dc=dc, bi=bi, tsl=tsl,
                              kc=kc, cols=cols: e.matmul(
                            k.bank(tsl, 512)[0:cols if cols < 128 else 128, :],
                            wdc[bi][:, kc * cols:(kc + 1) * cols],
                            S("xnT").rearrange("p (kc t) -> p kc t", kc=8)[:, kc, tsl * 512:(tsl + 1) * 512],
                            start=(kc == 0), stop=(kc == 7)))
            for tsl in range(4):
                b = tsl
                pem = k.mark("pe")
                if dc < 18:
                    k.wait("scalar", "pe", pem)
                    k.inc("scalar", "ac", 1, lambda e, dc=dc, tsl=tsl, b=b: e.activation(
                        xBC3[:, dc, 4 + tsl * 512: 4 + (tsl + 1) * 512],
                        k.bank(b, 512), AF.Copy))
                    k.bank_consumed(b, "ac")
                else:
                    k.wait("vector", "pe", pem)
                    k.inc("vector", "dv", 1, lambda e, tsl=tsl, b=b: e.tensor_copy(
                        S("dt_pre")[:, tsl * 512:(tsl + 1) * 512],
                        k.bank(b, 512)[0:NH, :]))
                    k.bank_consumed(b, "dv")
            if dc + 2 < 19:
                emit_wload(dc + 2, k.mark("pe"))
        k.inc("vector", "dv", 1, lambda e: e.memset(xBC3[:, :, 0:4], 0.0))
        k.inc("vector", "dv", 1, lambda e: e.memset(xBC3[:, :, 2052:2056], 0.0))
        k.tap("dt_pre", lambda: S("dt_pre"), [NH, T], F32, waits=("dv",))

    if BUILD_UPTO >= 2:
        ph2()
        phases_built.append(2)

    p3, _ = carve(nc, es, CARVE_OFFS["wdc0"], [
        ("sndEs", [128, 128], BF16), ("rcvEs", [128, 128], BF16),
    ])
    k.conv_scratch_base = CARVE_OFFS["dt_pre"]
    k.sb.update(p3)

    def ph3():
        xBC3 = S("xBC").rearrange("p (j c) -> p j c", c=TPAD)
        k.wait("vector", "ac")
        k.wait("vector", "dv")
        k.inc("vector", "dv", 1, lambda e: e.memset(S("sndEs"), 0.0))
        k.inc("vector", "dv", 1, lambda e: e.tensor_scalar(
            S("sndEs")[:, 0:54].rearrange("p (j c) -> p j c", c=3),
            xBC3[:, :, 2049:2052], S("flags")[:, 4:5], None, ALU.mult))
        k.inc("vector", "dv", 1, lambda e: e.tensor_scalar(
            S("sndEs")[:, 64:118].rearrange("p (j c) -> p j c", c=3),
            xBC3[:, :, 4:7], S("flags")[:, 5:6], None, ALU.mult))
        dvm = k.mark("dv")
        k.wait("scalar", "dv", dvm)
        qm = k.inc("scalar", "qsc", 16, lambda e: e.dma_start(
            out=k.dr["sndE"].ap(), in_=S("sndEs")))
        k.wait("gpsimd", "qsc", qm)
        k.inc("gpsimd", "cc", 1, lambda e: e.collective_compute(
            "AllReduce", ALU.add,
            replica_groups=[[0, 1], [2, 3], [4, 5], [6, 7]],
            ins=[k.dr["sndE"].ap().opt()], outs=[k.dr["rcvE"].ap().opt()]))
        ccm = k.mark("cc")
        k.wait("scalar", "cc", ccm)
        qm = k.inc("scalar", "qsc", 16, lambda e: e.dma_start(
            out=S("rcvEs"), in_=k.dr["rcvE"].ap()))
        k.wait("vector", "qsc", qm)
        k.inc("vector", "dv", 1, lambda e: e.tensor_scalar(
            xBC3[:, :, 1:4], S("rcvEs")[:, 0:54].rearrange("p (j c) -> p j c", c=3),
            S("flags")[:, 6:7], None, ALU.mult))
        k.inc("vector", "dv", 1, lambda e: e.tensor_scalar(
            xBC3[:, :, 2052:2055],
            S("rcvEs")[:, 64:118].rearrange("p (j c) -> p j c", c=3),
            S("flags")[:, 7:8], None, ALU.mult))
        k.xbc_patched = k.mark("dv")
        k.tap("xBC", lambda: S("xBC"), [128, 18 * TPAD], BF16, waits=("dv",))

    if BUILD_UPTO >= 3:
        ph3()
        phases_built.append(3)

    def ph4():
        for di, d in enumerate(("f", "r")):
            a = S(f"dtaln_{d}")[0:NH, :]
            b = S(f"dtaln_{d}")[NH:2 * NH, :]
            c = S("dt_pre")
            k.wait("vector", "dv")
            k.wait("vector", "ac")
            if di == 0:
                k.inc("vector", "dv", 1, lambda e: e.tensor_scalar(
                    S("dtaln_r")[0:NH, :], S("dt_pre"), S("dtbcol_r"), None, ALU.add))
                k.inc("vector", "dv", 1, lambda e, a=a: e.tensor_scalar(
                    a, S("dt_pre"), S("dtbcol_f"), None, ALU.add))
            dvm = k.mark("dv")
            k.wait("scalar", "dv", dvm)
            k.inc("scalar", "ac", 1, lambda e, a=a, c=c: e.activation(
                c, a, AF.Abs))
            k.inc("scalar", "ac", 1, lambda e, c=c: e.activation(
                c, c, AF.Exp, scale=-1.0))
            k.inc("scalar", "ac", 1, lambda e, c=c: e.activation(
                c, c, AF.Ln, bias=1.0))
            acm = k.mark("ac")
            k.wait("vector", "ac", acm)
            k.inc("vector", "dv", 1, lambda e, a=a: e.tensor_scalar(
                a, a, 0.0, None, ALU.max))
            k.inc("vector", "dv", 1, lambda e, a=a, c=c: e.tensor_tensor(
                a, a, c, ALU.add))
            dvm = k.mark("dv")
            k.wait("scalar", "dv", dvm)
            k.inc("scalar", "ac", 1, lambda e, a=a, b=b: e.activation(
                b, a, AF.Ln))
            k.wait("vector", "ac")
            k.inc("vector", "dv", 1, lambda e, a=a, d=d: e.tensor_scalar(
                a, a, S(f"nacol_{d}"), None, ALU.mult))
            k.tap(f"dtaln_{d}", lambda d=d: S(f"dtaln_{d}"), [64, T], F32,
                  waits=("dv", "ac"))

    if BUILD_UPTO >= 4:
        ph4()
        phases_built.append(4)

    p5, p5_end = carve(nc, es, k.conv_scratch_base, [
        ("fmA", [128, T], BF16), ("fmB", [128, T], BF16), ("fmC", [128, T], BF16),
    ])
    k.sb.update(p5)
    p5b, _ = carve(nc, es, TB, [
        ("xsj0", [128, DIN], BF16), ("xsj1", [128, DIN], BF16),
        ("xds", [128, DIN], BF16), ("slam", [128, DIN], BF16),
        ("t64s", [128, 64], F32), ("acumT", [128, 32], F32),
        ("dsc", [128, 32], F32),
    ])
    k.sb.update(p5b)

    def conv_dir(d):
        xBC3 = S("xBC").rearrange("p (j c) -> p j c", c=TPAD)
        if d == "f":
            taps = [(3, 4, False), (1, 2, False), (0, 1, False), (2, 3, False)]
        else:
            taps = [(3, 4, False), (1, 6, False), (0, 7, False), (2, 5, False)]
        fm_free_ac = getattr(k, "conv_fm_free_ac", 0)
        fmB_free = getattr(k, "conv_fmB_free", 0)
        fmC_free = getattr(k, "conv_fmC_free", 0)
        for j in range(18):
            k.wait("vector", "dv", k.xbc_patched)
            k.wait("vector", "ac", fm_free_ac)
            kk, off, _u = taps[0]
            k.inc("vector", "dv", 1, lambda e, j=j, kk=kk, off=off: e.tensor_scalar(
                S("fmA"), xBC3[:, j, off:off + T],
                S(f"convw_{d}")[:, j * 4 + kk:j * 4 + kk + 1],
                S(f"convb_{d}")[:, j:j + 1], ALU.mult, ALU.add))
            for kk, off, _u in taps[1:]:
                k.inc("vector", "dv", 1, lambda e, j=j, kk=kk, off=off:
                      e.scalar_tensor_tensor(
                          S("fmA"), xBC3[:, j, off:off + T],
                          S(f"convw_{d}")[:, j * 4 + kk:j * 4 + kk + 1],
                          S("fmA"), ALU.mult, ALU.add))
            dvm = k.mark("dv")
            if j < 16:
                dest = S("fmB")
            elif j == 16:
                dest = S(f"bcfm_{d}")[:, 0:T]
            else:
                dest = S(f"bcfm_{d}")[:, T:2 * T]
            k.wait("scalar", "dv", dvm)
            if j < 16:
                k.wait("scalar", "qsy", fmB_free)
            fm_free_ac = k.inc("scalar", "ac", 1, lambda e, dest=dest: e.activation(
                dest, S("fmA"), AF.Silu))
            if j < 16:
                k.wait("sync", "ac")
                k.wait("sync", "qsc", fmC_free)
                fmB_free = k.inc("sync", "qsy", 16, lambda e: e.dma_start(
                    out=S("fmC").rearrange("p (tc c) -> p tc c", c=128),
                    in_=S("fmB").rearrange("f (tc t) -> f tc t", tc=NCHUNK),
                    transpose=True))
                k.wait("scalar", "qsy")
                fmC_free = k.inc("scalar", "qsc", 16, lambda e, j=j, d=d: e.dma_start(
                    out=k.dr[f"xstd_{d}"].ap().rearrange(
                        "p (tc m) -> p tc m", m=DIN)[:, :, j * 128:(j + 1) * 128],
                    in_=S("fmC").rearrange("p (tc c) -> p tc c", c=128)))
            elif j == 16:
                k.wait("sync", "ac")
                k.inc("sync", "qsy", 16, lambda e, d=d: e.dma_start(
                    out=S(f"btok_{d}").rearrange("p (tc c) -> p tc c", c=128),
                    in_=S(f"bcfm_{d}")[:, 0:T].rearrange(
                        "f (tc t) -> f tc t", tc=NCHUNK), transpose=True))
        k.conv_fm_free_ac = fm_free_ac
        k.conv_fmB_free = fmB_free
        k.conv_fmC_free = fmC_free

    def pre_dir(d):
        order = range(NCHUNK) if d == "f" else range(NCHUNK - 1, -1, -1)
        ul = S("ulc") if d == "f" else S("urc")
        sel = S("sel127") if d == "f" else S("sel0")
        dtaln = S(f"dtaln_{d}")
        xsj = [S("xsj0"), S("xsj1")]
        k.wait("vector", "qsc")
        k.inc("vector", "dv", 1, lambda e: e.memset(S("s0t"), 0.0))
        xsj_free = [0, 0]
        for i, tc in enumerate(order):
            bi = i % 2
            k.wait("scalar", "qsc")
            k.wait("scalar", "dv", xsj_free[bi])
            qm = k.inc("scalar", "qsc", 16, lambda e, tc=tc, bi=bi, d=d: e.dma_start(
                out=xsj[bi], in_=k.dr[f"xstd_{d}"].ap()[:, tc * DIN:(tc + 1) * DIN]))
            k.bank_guard(0)
            k.inc("tensor", "pe", 1, lambda e, tc=tc, dtaln=dtaln: e.transpose(
                k.bank(0, 64), dtaln[:, tc * 128:(tc + 1) * 128],
                S("idf32")[0:64, 0:64]))
            pem = k.mark("pe")
            k.wait("vector", "pe", pem)
            k.tiny("vector", "dv", lambda e: e.tensor_copy(S("t64s"), k.bank(0, 64)))
            k.wait("tensor", "dv")
            k.inc("tensor", "pe", 1, lambda e, ul=ul: e.matmul(
                k.bank(0, 32, off=64), ul, S("t64s")[:, 0:32],
                start=True, stop=True))
            k.inc("tensor", "pe", 1, lambda e, ul=ul: e.matmul(
                k.bank(0, 128, off=96)[0:32, :], S("t64s")[:, 0:32], ul,
                start=True, stop=True))
            pem = k.mark("pe")
            k.wait("vector", "pe", pem)
            k.tiny("vector", "dv", lambda e: e.tensor_copy(
                S("acumT"), k.bank(0, 32, off=64)))
            k.wait("scalar", "pe", pem)
            k.wait("scalar", "dv")
            k.tiny("scalar", "ac", lambda e, tc=tc, d=d: e.activation(
                S(f"stack96_{d}")[0:32, tc * 128:(tc + 1) * 128],
                k.bank(0, 128, off=96)[0:32, :], AF.Copy))
            k.wait("vector", "ac")
            k.tiny("vector", "dv", lambda e, tc=tc, d=d: e.scalar_tensor_tensor(
                S(f"stack96_{d}")[32:64, tc * 128:(tc + 1) * 128],
                k.bank(0, 128, off=96)[0:32, :], 1.0,
                S(f"stack96_{d}")[0:32, tc * 128:(tc + 1) * 128],
                ALU.mult, ALU.subtract))
            k.tiny("vector", "dv", lambda e, tc=tc, d=d, dtaln=dtaln: e.tensor_copy(
                S(f"stack96_{d}")[64:96, tc * 128:(tc + 1) * 128],
                dtaln[32:64, tc * 128:(tc + 1) * 128]))
            k.wait("tensor", "dv")
            k.inc("tensor", "pe", 1, lambda e, sel=sel: e.matmul(
                k.bank(0, 32, off=224), sel, S("acumT"), start=True, stop=True))
            pem = k.mark("pe")
            k.wait("scalar", "pe", pem)
            k.tiny("scalar", "ac", lambda e, tc=tc, d=d: e.activation(
                S(f"lbce_{d}")[:, tc * 32:(tc + 1) * 32],
                k.bank(0, 32, off=224), AF.Exp))
            k.tiny("vector", "dv", lambda e: e.scalar_tensor_tensor(
                S("dsc"), S("t64s")[:, 32:64], 1.0, S("acumT"),
                ALU.mult, ALU.subtract))
            k.wait("vector", "pe", pem)
            k.tiny("vector", "dv", lambda e: e.tensor_tensor(
                S("dsc"), S("dsc"), k.bank(0, 32, off=224), ALU.add))
            k.bank_consumed(0, "dv")
            k.wait("scalar", "dv")
            k.tiny("scalar", "ac", lambda e, tc=tc, d=d: e.activation(
                S(f"ddt_{d}")[:, tc * 32:(tc + 1) * 32], S("dsc"), AF.Exp))
            k.wait("vector", "qsc", qm)
            k.wait("vector", "ac")
            k.inc("vector", "dv", 1, lambda e, tc=tc, bi=bi, d=d: e.tensor_tensor(
                S("xds").rearrange("p (h q) -> p h q", h=NH),
                xsj[bi].rearrange("p (h q) -> p h q", h=NH),
                S(f"ddt_{d}")[:, tc * 32:(tc + 1) * 32]
                    .unsqueeze(2).broadcast_to([128, NH, HD]), ALU.mult))
            xsj_free[bi] = k.cnt["dv"]
            dvm = k.mark("dv")
            k.wait("tensor", "dv", dvm)
            for q in range(4):
                k.bank_guard(4 + q)
                k.inc("tensor", "pe", 1, lambda e, q=q, tc=tc, d=d: e.matmul(
                    k.bank(4 + q, 512), S(f"btok_{d}")[:, tc * 128:(tc + 1) * 128],
                    S("xds")[:, q * 512:(q + 1) * 512], start=True, stop=True))
            pem = k.mark("pe")
            k.inc("vector", "dv", 1, lambda e, tc=tc, d=d: e.tensor_tensor(
                S("slam").rearrange("p (h q) -> p h q", h=NH),
                S("s0t").rearrange("p (h q) -> p h q", h=NH),
                S(f"lbce_{d}")[:, tc * 32:(tc + 1) * 32]
                    .unsqueeze(2).broadcast_to([128, NH, HD]), ALU.mult))
            k.wait("vector", "pe", pem)
            k.inc("vector", "dv", 1, lambda e: e.tensor_tensor(
                S("s0t"), k.ps.ap()[:, 2048:4096], S("slam"), ALU.add))
            for q in range(4):
                k.bank_consumed(4 + q, "dv")
        fcol = 0 if d == "f" else 1
        k.inc("vector", "dv", 1, lambda e, fcol=fcol: e.tensor_scalar(
            S("slam"), S("s0t"), S("flags")[:, fcol:fcol + 1], None, ALU.mult))
        dvm = k.mark("dv")
        k.wait("scalar", "dv", dvm)
        k.inc("scalar", "qsc", 16, lambda e, d=d: e.dma_start(
            out=k.dr["sndS"].ap()[:, (0 if d == "f" else DIN):
                                  (DIN if d == "f" else 2 * DIN)],
            in_=S("slam")))

    def ph5():
        conv_dir("f")
        conv_dir("r")
        k.barrier()
        pre_dir("f")
        pre_dir("r")
        k.tap("bcfm_f", lambda: S("bcfm_f"), [128, 2 * T], BF16, waits=("ac",))
        k.tap("btok_f", lambda: S("btok_f"), [128, T], BF16, waits=("qsy",))
        k.tap("stack96_f", lambda: S("stack96_f"), [96, NCHUNK * 128], BF16,
              waits=("dv", "ac"))
        k.tap("stack96_r", lambda: S("stack96_r"), [96, NCHUNK * 128], BF16,
              waits=("dv", "ac"))
        k.tap("lbce_f", lambda: S("lbce_f"), [128, NCHUNK * NH], BF16,
              waits=("ac",))
        k.tap("ddt_f", lambda: S("ddt_f"), [128, NCHUNK * NH], BF16,
              waits=("ac",))

    if BUILD_UPTO >= 5:
        ph5()
        phases_built.append(5)

    def ph6():
        k.wait("gpsimd", "qsc")
        k.inc("gpsimd", "cc", 1, lambda e: e.collective_compute(
            "AllReduce", ALU.add,
            replica_groups=[[0, 1], [2, 3], [4, 5], [6, 7]],
            ins=[k.dr["sndS"].ap().opt()], outs=[k.dr["rcvS"].ap().opt()]))

    if BUILD_UPTO >= 6:
        ph6()
        phases_built.append(6)

    p7, _ = carve(nc, es, TB, [
        ("wz", [128, 8 * DIN], BF16),
        ("szb0", [128, DIN], BF16), ("szb1", [128, DIN], BF16),
    ])
    k.sb.update(p7)

    def ph7():
        k.barrier()
        wzq = k.inc("gpsimd", "qgp", 16, lambda e: e.dma_start(
            out=S("wz").rearrange("p (kc m) -> p kc m", kc=8),
            in_=DI["winT"][:, 0:DIN].rearrange("(kc p) m -> p kc m", p=128)))
        k.wait("tensor", "qgp", wzq)
        szb = [S("szb0"), S("szb1")]
        szb_free = [0, 0]
        xnT3 = S("xnT").rearrange("p (kc t) -> p kc t", kc=8)
        for tc in range(NCHUNK):
            bi = tc % 2
            bank0 = 0 if bi == 0 else 4
            for zsl in range(4):
                k.bank_guard(bank0 + zsl)
            for kc in range(8):
                for zsl in range(4):
                    k.inc("tensor", "pe", 1, lambda e, tc=tc, zsl=zsl, kc=kc,
                          b=bank0 + zsl: e.matmul(
                        k.bank(b, 512),
                        xnT3[:, kc, tc * 128:(tc + 1) * 128],
                        S("wz")[:, kc * DIN + zsl * 512: kc * DIN + (zsl + 1) * 512],
                        start=(kc == 0), stop=(kc == 7)))
            pem = k.mark("pe")
            for zsl in range(4):
                k.wait("scalar", "pe", pem)
                k.wait("scalar", "qsc", szb_free[bi])
                k.inc("scalar", "ac", 1, lambda e, zsl=zsl, bi=bi,
                      b=bank0 + zsl: e.activation(
                    szb[bi][:, zsl * 512:(zsl + 1) * 512], k.bank(b, 512), AF.Silu))
                k.bank_consumed(bank0 + zsl, "ac")
            acm = k.mark("ac")
            k.wait("scalar", "ac", acm)
            szb_free[bi] = k.inc("scalar", "qsc", 16, lambda e, tc=tc, bi=bi:
                                 e.dma_start(
                out=k.dr["siluzd"].ap()[:, tc * DIN:(tc + 1) * DIN], in_=szb[bi]))
        k.wait("scalar", "cc")
        q1 = k.inc("scalar", "qsc", 16, lambda e: e.dma_start(
            out=S("szb0"), in_=k.dr["rcvS"].ap()[:, 0:DIN]))
        q2 = k.inc("scalar", "qsc", 16, lambda e: e.dma_start(
            out=S("szb1"), in_=k.dr["rcvS"].ap()[:, DIN:2 * DIN]))
        k.wait("vector", "qsc", q2)
        k.inc("vector", "dv", 1, lambda e: e.tensor_scalar(
            S("sin_f"), S("szb0"), S("flags")[:, 2:3], None, ALU.mult))
        k.inc("vector", "dv", 1, lambda e: e.tensor_scalar(
            S("sin_r"), S("szb1"), S("flags")[:, 3:4], None, ALU.mult))
        k.tap("sin_f", lambda: S("sin_f"), [128, DIN], BF16, waits=("dv",))
        k.tap("sin_r", lambda: S("sin_r"), [128, DIN], BF16, waits=("dv",))

    if BUILD_UPTO >= 7:
        ph7()
        phases_built.append(7)

    p8, p8_end = carve(nc, es, TB, [
        ("Ebuf", [128, 4096], BF16), ("aEbuf", [128, 4096], BF16),
        ("rowbuf", [2, 4096], BF16),
        ("mxs0", [128, DIN], BF16), ("mxs1", [128, DIN], BF16),
        ("msz0", [128, DIN], BF16), ("msz1", [128, DIN], BF16),
        ("my30", [128, DIN], BF16), ("my31", [128, DIN], BF16),
        ("mg1", [128, 1024], BF16), ("mg2", [128, DIN], BF16),
        ("myg", [128, DIN], BF16), ("msq", [128, DIN], BF16),
        ("mxds", [128, DIN], BF16), ("mslam", [128, DIN], BF16),
        ("mysb0", [128, DIN], BF16), ("mysb1", [128, DIN], BF16),
        ("mscoresT", [128, 128], BF16),
        ("gcol", [128, 8], F32), ("gcolB", [128, 4], F32),
        ("nblf", [128, 512], BF16), ("nblr", [128, 512], BF16),
    ])
    k.sb.update(p8)

    def main_dir(d):
        order = range(NCHUNK) if d == "f" else range(NCHUNK - 1, -1, -1)
        nbl = S("nblf") if d == "f" else S("nblr")
        xsj = [S("mxs0"), S("mxs1")]
        szj = [S("msz0"), S("msz1")]
        y3j = [S("my30"), S("my31")]
        ysb = [S("mysb0"), S("mysb1")]
        xsj_free = getattr(k, "m_xsj_free", [0, 0])
        szj_free = getattr(k, "m_szj_free", [0, 0])
        y3j_free = getattr(k, "m_y3j_free", [0, 0])
        ysb_free = getattr(k, "m_ysb_free", [0, 0])
        k.wait("vector", "dv")
        k.wait("vector", "pe")
        k.inc("vector", "dv", 1, lambda e, d=d: e.tensor_copy(
            S("s0t"), S(f"sin_{d}")))
        rowbuf_free = getattr(k, "m_rowbuf_free", 0)

        def emit_block(cx):
            tc, bi = cx["tc"], cx["bi"]
            plm = cx["plm"]; dvm_m2 = cx["dvm_m2"]
            for hh in range(2):
                k.bank_guard(4); k.bank_guard(5)
                if hh == 0:
                    k.wait("tensor", "dv", cx["m2h_dv"])
                    k.wait("tensor", "pl", cx["m2h_pl"])
                else:
                    k.wait("tensor", "dv", dvm_m2)
                    k.wait("tensor", "pl", plm)
                for hl in range(16):
                    hg = hh * 16 + hl
                    bnk = 4 + hl // 8
                    k.inc("tensor", "pe", 1, lambda e, hg=hg, hl=hl, bnk=bnk,
                          bi=bi: e.matmul(
                        k.bank(bnk, 64, off=(hl % 8) * 64),
                        S("Ebuf")[:, hg * 128:(hg + 1) * 128],
                        xsj[bi][:, hg * 64:(hg + 1) * 64],
                        start=(hl % 8 == 0), stop=False))
                for hl in range(16):
                    hg = hh * 16 + hl
                    bnk = 4 + hl // 8
                    k.inc("tensor", "pe", 1, lambda e, hg=hg, hl=hl,
                          bnk=bnk: e.matmul(
                        k.bank(bnk, 64, off=(hl % 8) * 64),
                        S("aEbuf")[:, hg * 128:(hg + 1) * 128],
                        S("s0t")[:, hg * 64:(hg + 1) * 64],
                        start=False, stop=(hl % 8 == 7)))
                k.bank_guard(6); k.bank_guard(7)
                k.wait("tensor", "pl", cx["pl_xds"])
                for q in range(2):
                    k.inc("tensor", "pe", 1, lambda e, q=q, hh=hh, tc=tc,
                          d=d: e.matmul(
                        k.bank(6 + q, 512),
                        S(f"btok_{d}")[:, tc * 128:(tc + 1) * 128],
                        S("mxds")[:, hh * 1024 + q * 512:hh * 1024 + (q + 1) * 512],
                        start=True, stop=True))
                pem = k.mark("pe")
                k.wait("gpsimd", "qsc", cx["qx"])
                k.wait("gpsimd", "dv")
                k.inc("gpsimd", "pl", 1, lambda e, hh=hh, bi=bi, d=d: e.tensor_tensor(
                    S("mg1").rearrange("p (h q) -> p h q", h=16),
                    xsj[bi][:, hh * 1024:(hh + 1) * 1024]
                        .rearrange("p (h q) -> p h q", h=16),
                    S(f"dbc_{d}")[:, hh * 16:(hh + 1) * 16]
                        .unsqueeze(2).broadcast_to([128, 16, HD]), ALU.mult))
                k.wait("vector", "pe", pem)
                k.wait("vector", "pl")
                k.wait("vector", "qsc", getattr(k, "mg2_hold", 0))
                k.inc("vector", "dv", 1, lambda e, hh=hh: e.scalar_tensor_tensor(
                    S("mg2")[:, hh * 1024:(hh + 1) * 1024],
                    k.ps.ap()[:, 4 * 512:6 * 512], 1.0, S("mg1"),
                    ALU.mult, ALU.add))
                k.bank_consumed(4, "dv"); k.bank_consumed(5, "dv")
                k.wait("gpsimd", "pe", pem)
                k.inc("gpsimd", "pl", 1, lambda e, hh=hh, tc=tc, d=d: e.tensor_tensor(
                    S("mslam")[:, hh * 1024:(hh + 1) * 1024]
                        .rearrange("p (h q) -> p h q", h=16),
                    S("s0t")[:, hh * 1024:(hh + 1) * 1024]
                        .rearrange("p (h q) -> p h q", h=16),
                    S(f"lbce_{d}")[:, tc * 32 + hh * 16:tc * 32 + (hh + 1) * 16]
                        .unsqueeze(2).broadcast_to([128, 16, HD]), ALU.mult))
                k.wait("vector", "pl")
                k.inc("vector", "dv", 1, lambda e, hh=hh: e.tensor_tensor(
                    S("s0t")[:, hh * 1024:(hh + 1) * 1024],
                    k.ps.ap()[:, 6 * 512:8 * 512],
                    S("mslam")[:, hh * 1024:(hh + 1) * 1024], ALU.add))
                k.bank_consumed(6, "dv"); k.bank_consumed(7, "dv")
            k.mm_prev_ydyo = k.mark("pe")
            k.wait("vector", "qsc", cx["qz"])
            k.inc("vector", "dv", 1, lambda e, bi=bi: e.tensor_tensor(
                S("myg"), S("mg2"), szj[bi], ALU.mult))
            szj_free[bi] = k.cnt["dv"]
            k.wait("vector", "qsc", getattr(k, "msq_hold", 0))
            k.inc("vector", "dv", 1, lambda e: e.tensor_tensor(
                S("msq"), S("myg"), S("myg"), ALU.mult))
            k.tiny("vector", "dv", lambda e: e.reduce_sum(
                S("gcol")[:, 0:1], S("msq"), axis=mybir.AxisListType.X))
            k.tiny("vector", "dv", lambda e: e.tensor_scalar(
                S("gcol")[:, 1:2], S("gcol")[:, 0:1], 1.0 / DIN, EPS,
                ALU.mult, ALU.add))
            k.wait("scalar", "dv")
            k.tiny("scalar", "ac", lambda e: e.activation(
                S("gcolB")[:, 0:1], S("gcol")[:, 1:2], AF.Sqrt))
            k.wait("vector", "ac")
            k.tiny("vector", "dv", lambda e: e.reciprocal(
                S("gcol")[:, 2:3], S("gcolB")[:, 0:1]))
            k.inc("vector", "dv", 1, lambda e, d=d: e.scalar_tensor_tensor(
                S("msq"), S("myg"), S("gcol")[:, 2:3], S(f"nwbc_{d}"),
                ALU.mult, ALU.mult))
            xsj_free[bi] = k.cnt["dv"]
            if d == "f":
                dvm = k.mark("dv")
                k.wait("scalar", "dv", dvm)
                k.msq_hold = k.inc("scalar", "qsc", 16, lambda e, tc=tc: e.dma_start(
                    out=k.dr["y3fd"].ap()[:, tc * DIN:(tc + 1) * DIN],
                    in_=S("msq")))
            else:
                k.wait("vector", "qsc", cx["qy3"])
                k.wait("vector", "qsc", ysb_free[bi])
                k.inc("vector", "dv", 1, lambda e, bi=bi: e.tensor_tensor(
                    ysb[bi], S("msq"), y3j[bi], ALU.add))
                y3j_free[bi] = k.cnt["dv"]
                dvm = k.mark("dv")
                k.wait("scalar", "dv", dvm)
                ysb_free[bi] = k.inc("scalar", "qsc", 16, lambda e, tc=tc,
                                     bi=bi: e.dma_start(
                    out=k.dr["ysumd"].ap()[:, tc * DIN:(tc + 1) * DIN],
                    in_=ysb[bi]))
            if f"ypred_{d}" in DEBUG_TAPS:
                dvm = k.mark("dv")
                k.wait("scalar", "dv", dvm)
                k.mg2_hold = k.inc("scalar", "qsc", 16,
                                   lambda e, tc=tc, d=d: e.dma_start(
                    out=k.dr[f"ypred_{d}"].ap()[:, tc * DIN:(tc + 1) * DIN],
                    in_=S("mg2")))

        prev = None
        for i, tc in enumerate(order):
            if MAIN_LIMIT is not None and i >= MAIN_LIMIT:
                break
            bi = i % 2
            k.wait("scalar", "dv", xsj_free[bi])
            qx = k.inc("scalar", "qsc", 16, lambda e, tc=tc, bi=bi, d=d: e.dma_start(
                out=xsj[bi], in_=k.dr[f"xstd_{d}"].ap()[:, tc * DIN:(tc + 1) * DIN]))
            k.wait("scalar", "dv", szj_free[bi])
            qz = k.inc("scalar", "qsc", 16, lambda e, tc=tc, bi=bi: e.dma_start(
                out=szj[bi], in_=k.dr["siluzd"].ap()[:, tc * DIN:(tc + 1) * DIN]))
            qy3 = 0
            if d == "r":
                k.wait("scalar", "dv", y3j_free[bi])
                qy3 = k.inc("scalar", "qsc", 16, lambda e, tc=tc, bi=bi: e.dma_start(
                    out=y3j[bi], in_=k.dr["y3fd"].ap()[:, tc * DIN:(tc + 1) * DIN]))
            k.wait("sync", "pe", rowbuf_free)
            qr = k.inc("sync", "qsy", 16, lambda e, tc=tc, d=d: e.dma_start(
                out=S("rowbuf"),
                in_=S(f"stack96_{d}")[0:64, tc * 128:(tc + 1) * 128]))
            k.bank_guard(0)
            k.inc("tensor", "pe", 1, lambda e, tc=tc, d=d: e.matmul(
                k.bank(0, 128), S(f"bcfm_{d}")[:, tc * 128:(tc + 1) * 128],
                S(f"bcfm_{d}")[:, T + tc * 128:T + (tc + 1) * 128],
                start=True, stop=True))
            pem = k.mark("pe")
            k.wait("scalar", "pe", pem)
            k.wait("scalar", "dv", prev["dvm_m2"] if prev is not None
                   else getattr(k, "m_last_m2", 0))
            k.inc("scalar", "ac", 1, lambda e: e.activation(
                S("mscoresT"), k.bank(0, 128), AF.Copy))
            k.bank_consumed(0, "ac")
            if prev is not None:
                emit_block(prev)
            ydm = getattr(k, "mm_prev_ydyo", 0)
            k.wait("scalar", "pe", ydm)
            k.wait("tensor", "qsy", qr)
            for q8 in range(8):
                ab = 1 + (q8 % 2)
                k.bank_guard(ab)
                k.inc("tensor", "pe", 1, lambda e, q8=q8, ab=ab: e.matmul(
                    k.bank(ab, 512), S("ones2"),
                    S("rowbuf")[:, q8 * 512:(q8 + 1) * 512],
                    start=True, stop=False))
                k.inc("tensor", "pe", 1, lambda e, q8=q8, ab=ab, tc=tc, d=d: e.matmul(
                    k.bank(ab, 512).rearrange("p (h l) -> p h l", h=4),
                    S(f"stack96_{d}")[0:96, tc * 128:(tc + 1) * 128],
                    S("ohc")[:, q8 * 4:(q8 + 1) * 4]
                        .unsqueeze(2).broadcast_to([96, 4, 128]),
                    start=False, stop=False))
                k.inc("tensor", "pe", 1, lambda e, q8=q8, ab=ab, nbl=nbl: e.matmul(
                    k.bank(ab, 512), S("idbf"), nbl,
                    start=False, stop=True))
                ab3 = 3 if q8 % 2 == 0 else 0
                k.bank_guard(ab3)
                k.inc("tensor", "pe", 1, lambda e, q8=q8, ab3=ab3: e.matmul(
                    k.bank(ab3, 512), S("ones2"),
                    S("rowbuf")[:, q8 * 512:(q8 + 1) * 512],
                    start=True, stop=True))
                pem = k.mark("pe")
                if q8 == 7:
                    rowbuf_free = pem
                k.wait("scalar", "pe", pem)
                k.inc("scalar", "ac", 1, lambda e, q8=q8, ab=ab: e.activation(
                    S("Ebuf")[:, q8 * 512:(q8 + 1) * 512], k.bank(ab, 512), AF.Exp))
                k.bank_consumed(ab, "ac")
                k.inc("scalar", "ac", 1, lambda e, q8=q8, ab3=ab3: e.activation(
                    S("aEbuf")[:, q8 * 512:(q8 + 1) * 512], k.bank(ab3, 512), AF.Exp))
                k.bank_consumed(ab3, "ac")
                acm = k.mark("ac")
                k.wait("vector", "ac", acm)
                k.inc("vector", "dv", 1, lambda e, q8=q8: e.tensor_tensor(
                    S("Ebuf")[:, q8 * 512:(q8 + 1) * 512]
                        .rearrange("p (h l) -> p h l", h=4),
                    S("Ebuf")[:, q8 * 512:(q8 + 1) * 512]
                        .rearrange("p (h l) -> p h l", h=4),
                    S("mscoresT").unsqueeze(1).broadcast_to([128, 4, 128]),
                    ALU.mult))
                k.wait("gpsimd", "ac", acm)
                k.inc("gpsimd", "pl", 1, lambda e, q8=q8, tc=tc, d=d: e.tensor_tensor(
                    S("aEbuf")[:, q8 * 512:(q8 + 1) * 512]
                        .rearrange("p (h l) -> p h l", h=4),
                    S(f"bcfm_{d}")[:, T + tc * 128:T + (tc + 1) * 128]
                        .unsqueeze(1).broadcast_to([128, 4, 128]),
                    S("aEbuf")[:, q8 * 512:(q8 + 1) * 512]
                        .rearrange("p (h l) -> p h l", h=4),
                    ALU.mult))
                if q8 == 3:
                    k.m2_half_dv = k.mark("dv")
                    k.m2_half_pl = k.mark("pl")
            k.wait("gpsimd", "qsc", qx)
            k.wait("gpsimd", "pe", ydm)
            k.inc("gpsimd", "pl", 1, lambda e, tc=tc, bi=bi, d=d: e.tensor_tensor(
                S("mxds").rearrange("p (h q) -> p h q", h=NH),
                xsj[bi].rearrange("p (h q) -> p h q", h=NH),
                S(f"ddt_{d}")[:, tc * 32:(tc + 1) * 32]
                    .unsqueeze(2).broadcast_to([128, NH, HD]), ALU.mult))
            prev = dict(tc=tc, bi=bi, qx=qx, qz=qz, qy3=qy3,
                        dvm_m2=k.mark("dv"), plm=k.mark("pl"),
                        m2h_dv=k.m2_half_dv, m2h_pl=k.m2_half_pl,
                        pl_xds=k.mark("pl"))
            k.m_last_m2 = prev["dvm_m2"]
            k.m_rowbuf_free = rowbuf_free
            k.m_xsj_free = xsj_free; k.m_szj_free = szj_free
            k.m_y3j_free = y3j_free; k.m_ysb_free = ysb_free
        if prev is not None:
            emit_block(prev)
        k.m_rowbuf_free = rowbuf_free
        k.m_xsj_free = xsj_free; k.m_szj_free = szj_free
        k.m_y3j_free = y3j_free; k.m_ysb_free = ysb_free

    def ph8_taps():
        k.tap("dbc_f", lambda: S("dbc_f"), [128, NH], F32, waits=("dv",))
        k.tap("dbc_r", lambda: S("dbc_r"), [128, NH], F32, waits=("dv",))
        k.tap("Ebuf", lambda: S("Ebuf"), [128, 4096], BF16, waits=("dv", "ac"))
        k.tap("aEbuf", lambda: S("aEbuf"), [128, 4096], BF16, waits=("dv", "ac"))
        k.tap("rowbuf", lambda: S("rowbuf"), [2, 4096], BF16, waits=("qsy",))
        k.tap("mscoresT", lambda: S("mscoresT"), [128, 128], BF16, waits=("ac",))
        k.tap("mg2", lambda: S("mg2"), [128, DIN], BF16, waits=("dv",))

    def ph8():
        k.barrier()
        nq = k.inc("gpsimd", "qgp", 16, lambda e: e.dma_start(
            out=S("nblf"), in_=DI["nblfc"]))
        nq = k.inc("gpsimd", "qgp", 16, lambda e: e.dma_start(
            out=S("nblr"), in_=DI["nblrc"]))
        k.wait("tensor", "qgp", nq)
        for d in MAIN_DIRS:
            if f"ypred_{d}" in DEBUG_TAPS:
                dscratch(f"ypred_{d}", [128, NCHUNK * DIN], BF16)
            main_dir(d)
        ph8_taps()

    if BUILD_UPTO >= 8:
        ph8()
        phases_built.append(8)

    p9, _ = carve(nc, es, TB, [
        ("ysumFM", [128, NCHUNK * DIN], BF16),
        ("ych0", [128, DIN], BF16), ("ych1", [128, DIN], BF16),
    ])
    k.sb.update(p9)

    def ph9():
        k.barrier()
        ych = [S("ych0"), S("ych1")]
        yFM3 = S("ysumFM").rearrange("p (fc t) -> p fc t", fc=16)
        ych_free = [0, 0]
        for tc in range(NCHUNK):
            bi = tc % 2
            k.wait("scalar", "qsy", ych_free[bi])
            qm = k.inc("scalar", "qsc", 16, lambda e, tc=tc, bi=bi: e.dma_start(
                out=ych[bi], in_=k.dr["ysumd"].ap()[:, tc * DIN:(tc + 1) * DIN]))
            k.wait("sync", "qsc", qm)
            ych_free[bi] = k.inc("sync", "qsy", 16, lambda e, tc=tc, bi=bi:
                                 e.dma_start(
                out=yFM3[:, :, tc * 128:(tc + 1) * 128],
                in_=ych[bi].rearrange("p (fc f) -> p fc f", fc=16),
                transpose=True))

    if BUILD_UPTO >= 9:
        ph9()
        phases_built.append(9)

    DEAD = CARVE_OFFS["stack96_f"]
    p10, _ = carve(nc, es, DEAD, [
        ("woutTs", [128, NCHUNK * 1024], BF16),
        ("xfm0", [128, T], F32), ("xfm1", [128, T], F32),
        ("wevac", [128, T], F32),
    ])
    k.sb.update(p10)

    def ph10():
        wq = k.inc("gpsimd", "qgp", 16, lambda e: e.dma_start(
            out=S("woutTs").rearrange("p (fc m) -> p fc m", fc=16),
            in_=DI["woutT"].rearrange("(fc p) m -> p fc m", p=128)))
        k.wait("tensor", "qgp", wq)
        xfm = [S("xfm0"), S("xfm1")]
        xfm_free = [0, 0]
        wevac_hold = 0
        for do in range(8):
            bi = do % 2
            k.wait("scalar", "dv", xfm_free[bi])
            qx = k.inc("scalar", "qsc", 16, lambda e, do=do, bi=bi: e.dma_start(
                out=xfm[bi], in_=DI["xfm"][do * 128:(do + 1) * 128, :]))
            base = (do % 2) * 4
            for q in range(4):
                k.bank_guard(base + q)
            k.wait("tensor", "qsy")
            for fc in range(NCHUNK):
                for tq in range(4):
                    k.inc("tensor", "pe", 1, lambda e, do=do, fc=fc, tq=tq,
                          base=base: e.matmul(
                        k.bank(base + tq, 512),
                        S("woutTs")[:, fc * 1024 + do * 128: fc * 1024 + (do + 1) * 128],
                        S("ysumFM")[:, fc * DIN + tq * 512: fc * DIN + (tq + 1) * 512],
                        start=(fc == 0), stop=(fc == NCHUNK - 1)))
            pem = k.mark("pe")
            k.wait("vector", "pe", pem)
            k.wait("vector", "qsc", qx)
            k.wait("vector", "qgp", wevac_hold)
            for tq in range(4):
                k.inc("vector", "dv", 1, lambda e, tq=tq, bi=bi, base=base:
                      e.scalar_tensor_tensor(
                    S("wevac")[:, tq * 512:(tq + 1) * 512],
                    k.bank(base + tq, 512), 1.0,
                    xfm[bi][:, tq * 512:(tq + 1) * 512], ALU.mult, ALU.add))
                k.bank_consumed(base + tq, "dv")
            xfm_free[bi] = k.cnt["dv"]
            dvm = k.mark("dv")
            k.wait("gpsimd", "dv", dvm)
            wevac_hold = k.inc("gpsimd", "qgp", 16, lambda e, do=do: e.dma_start(
                out=k.dr["accd"].ap()[:, do * T:(do + 1) * T], in_=S("wevac")))

    if BUILD_UPTO >= 10:
        ph10()
        phases_built.append(10)

    p11a, _ = carve(nc, es, TB, [
        ("fuse", [128, 16 * T], BF16),
        ("wbufA0", [128, 8 * 1024], BF16), ("wbufA1", [128, 8 * 1024], BF16),
        ("wT8", [8, NCHUNK * 128], BF16),
    ])
    p11b, _ = carve(nc, es, DEAD, [
        ("dfull", [128, 16 * 1024], BF16),
        ("sgb", [128, T], BF16), ("webc", [128, T], BF16),
        ("fevac0", [128, T], F32), ("fevac1", [128, T], F32),
    ], limit=TB)
    k.sb.update(p11a); k.sb.update(p11b)

    def ph11():
        k.barrier()
        for tc in range(NCHUNK):
            k.bank_guard(0)
            k.inc("tensor", "pe", 1, lambda e, tc=tc: e.transpose(
                k.bank(0, 128)[0:8, :], S("fullw")[:, tc * 8:(tc + 1) * 8],
                S("idf32")))
            pem = k.mark("pe")
            k.wait("scalar", "pe", pem)
            k.inc("scalar", "ac", 1, lambda e, tc=tc: e.activation(
                S("wT8")[:, tc * 128:(tc + 1) * 128], k.bank(0, 128)[0:8, :],
                AF.Copy))
            k.bank_consumed(0, "ac")
        wA = [S("wbufA0"), S("wbufA1")]
        wA_free = [0, 0]
        dq_hold = 0
        fevac_hold2 = [0, 0]
        xnT3 = S("xnT").rearrange("p (kc t) -> p kc t", kc=8)
        for e_ in range(NE):
            for q in range(4):
                k.bank_guard(4 + q)
                k.inc("tensor", "pe", 1, lambda e, e_=e_, q=q: e.matmul(
                    k.bank(4 + q, 512), S("sel8")[:, e_ * 128:(e_ + 1) * 128],
                    S("wT8")[:, q * 512:(q + 1) * 512], start=True, stop=True))
            pem = k.mark("pe")
            k.wait("scalar", "pe", pem)
            k.wait("scalar", "dv")
            for q in range(4):
                k.inc("scalar", "ac", 1, lambda e, q=q: e.activation(
                    S("webc")[:, q * 512:(q + 1) * 512], k.bank(4 + q, 512),
                    AF.Copy))
                k.bank_consumed(4 + q, "ac")
            for Fq in range(4):
                if Fq == 3:
                    k.wait("gpsimd", "pe")
                    dq_hold = k.inc("gpsimd", "qgp", 16,
                                    lambda e, e_=e_: e.dma_start(
                        out=S("dfull").rearrange("p (fc m) -> p fc m", fc=16),
                        in_=DI["dEFD"][e_].rearrange("(fc p) m -> p fc m", p=128)))
                bi = Fq % 2
                k.wait("gpsimd", "pe", wA_free[bi])
                qmf = []
                for fcq in range(4):
                    col = Fq * 512 + fcq * 128
                    for gu, nm in ((0, "gEDF"), (1, "uEDF")):
                        qm = k.inc("gpsimd", "qgp", 16,
                                   lambda e, e_=e_, col=col, bi=bi, gu=gu, nm=nm:
                                   e.dma_start(
                            out=wA[bi][:, gu * 4096:(gu + 1) * 4096]
                                .rearrange("p (kc m) -> p kc m", kc=8)
                                [:, :, (col % 512):(col % 512) + 128],
                            in_=DI[nm][e_][:, col:col + 128]
                                .rearrange("(kc p) m -> p kc m", p=128)))
                    qmf.append(qm)
                for fcq in range(4):
                    k.wait("tensor", "qgp", qmf[fcq])
                    FCg = Fq * 4 + fcq
                    for th in range(2):
                        gi = getattr(k, "ffn_gi", 0)
                        k.ffn_gi = gi + 1
                        base = (gi % 2) * 4
                        for q in range(4):
                            k.bank_guard(base + q)
                        for kc in range(8):
                            for gu in range(2):
                                for ts_ in range(2):
                                    tcol = th * 1024 + ts_ * 512
                                    woff = gu * 4096 + kc * 512 + fcq * 128
                                    k.inc("tensor", "pe", 1, lambda e, kc=kc,
                                          ts_=ts_, gu=gu, bi=bi, base=base,
                                          tcol=tcol, woff=woff: e.matmul(
                                        k.bank(base + gu * 2 + ts_, 512),
                                        wA[bi][:, woff:woff + 128],
                                        xnT3[:, kc, tcol:tcol + 512],
                                        start=(kc == 0), stop=(kc == 7)))
                        pem = k.mark("pe")
                        if fcq == 3 and th == 1:
                            wA_free[bi] = pem
                        k.wait("scalar", "pe", pem)
                        k.wait("scalar", "dv")
                        k.inc("scalar", "ac", 1, lambda e, th=th, base=base:
                              e.activation(
                            S("sgb")[:, th * 1024:(th + 1) * 1024],
                            k.ps.ap()[:, base * 512: base * 512 + 1024], AF.Silu))
                        acm = k.mark("ac")
                        k.bank_consumed(base, "ac"); k.bank_consumed(base + 1, "ac")
                        k.wait("vector", "ac", acm)
                        k.inc("vector", "dv", 1, lambda e, th=th: e.tensor_tensor(
                            S("sgb")[:, th * 1024:(th + 1) * 1024],
                            S("sgb")[:, th * 1024:(th + 1) * 1024],
                            S("webc")[:, th * 1024:(th + 1) * 1024], ALU.mult))
                        k.inc("vector", "dv", 1, lambda e, FCg=FCg, th=th,
                              base=base: e.scalar_tensor_tensor(
                            S("fuse")[:, FCg * T + th * 1024: FCg * T + (th + 1) * 1024],
                            k.ps.ap()[:, (base + 2) * 512: (base + 2) * 512 + 1024],
                            1.0, S("sgb")[:, th * 1024:(th + 1) * 1024],
                            ALU.mult, ALU.mult))
                        k.bank_consumed(base + 2, "dv"); k.bank_consumed(base + 3, "dv")
            k.wait("tensor", "qgp", dq_hold)
            k.wait("tensor", "dv")
            for do in range(8):
                base = (do % 2) * 4
                for q in range(4):
                    k.bank_guard(base + q)
                for fc in range(NCHUNK):
                    for tq in range(4):
                        k.inc("tensor", "pe", 1, lambda e, do=do, fc=fc, tq=tq,
                              base=base: e.matmul(
                            k.bank(base + tq, 512),
                            S("dfull")[:, fc * 1024 + do * 128:
                                       fc * 1024 + (do + 1) * 128],
                            S("fuse")[:, fc * T + tq * 512: fc * T + (tq + 1) * 512],
                            start=(fc == 0), stop=(fc == NCHUNK - 1)))
                pem = k.mark("pe")
                fb = do % 2
                fev = S("fevac0") if fb == 0 else S("fevac1")
                k.wait("vector", "pe", pem)
                k.wait("vector", "qgp", fevac_hold2[fb])
                for tq in range(4):
                    k.inc("vector", "dv", 1, lambda e, tq=tq, base=base, fev=fev:
                          e.tensor_copy(
                        fev[:, tq * 512:(tq + 1) * 512],
                        k.bank(base + tq, 512)))
                    k.bank_consumed(base + tq, "dv")
                dvm = k.mark("dv")
                k.wait("gpsimd", "dv", dvm)
                fevac_hold2[fb] = k.inc("gpsimd", "qgp", 16, lambda e, do=do,
                                        fev=fev: e.dma_start(
                    out=k.dr["accd"].ap()[:, do * T:(do + 1) * T],
                    in_=fev, accum_op=ALU.add))

    if BUILD_UPTO >= 11:
        ph11()
        phases_built.append(11)

    def ph12():
        k.wait("gpsimd", "qgp")
        k.wait("scalar", "qgp")
        k.inc("gpsimd", "qgp", 16, lambda e: e.dma_start(
            out=k.dr["ofm"].ap()[:, 0:4 * T], in_=k.dr["accd"].ap()[:, 0:4 * T]))
        k.inc("scalar", "qsc", 16, lambda e: e.dma_start(
            out=k.dr["ofm"].ap()[:, 4 * T:8 * T],
            in_=k.dr["accd"].ap()[:, 4 * T:8 * T]))
        k.wait("gpsimd", "qgp")
        k.wait("gpsimd", "qsc")

    if BUILD_UPTO >= 12:
        ph12()
        phases_built.append(12)

    if k.tap0 is not None:
        k.wait("scalar", "ac")
        k.wait("scalar", "dv")
        k.inc("scalar", "qsc", 16, lambda e: e.dma_start(
            out=k.tap0.ap()[:, 0:1], in_=k.c0ap, allow_slow_non_contiguous=True))
        k.inc("scalar", "qsc", 16, lambda e: e.dma_start(
            out=k.tap0.ap()[:, 1:2], in_=k.c1ap, allow_slow_non_contiguous=True))

    return nc, es, k, DI, phases_built


def finalize(nc, es, k):
    k.wait("sync", "qsc")
    k.wait("sync", "qsy")
    k.wait("sync", "qgp")
    k.wait("sync", "cc")
    k.wait("sync", "pe")
    k.wait("sync", "dv")
    k.wait("sync", "ac")
    k.wait("sync", "pl")
    with nc.Block() as block:
        for eng in Bld.ENGS:
            fns = k.ops[eng]
            if not fns:
                continue
            getattr(block, eng)(lambda e, fns=fns: [fn(e) for fn in fns])
    es.close()
    return nc


def get_program():
    key = (BUILD_UPTO, frozenset(DEBUG_TAPS))
    if key not in _cache:
        nc, es, k, DI, ph = build_program()
        finalize(nc, es, k)
        _cache[key] = nc
    return _cache[key]


def _host_consts():
    f = np.float32
    idx = np.arange(128)
    UL = (idx[:, None] <= idx[None, :]).astype(f)
    UR = (idx[:, None] >= idx[None, :]).astype(f)
    sel127 = np.zeros((128, 128), f); sel127[127, :] = 1.0
    sel0 = np.zeros((128, 128), f); sel0[0, :] = 1.0
    idf = np.eye(128, dtype=f)
    ones2 = np.ones((2, 128), f)
    ohc = np.concatenate([-np.eye(NH, dtype=f), -np.eye(NH, dtype=f),
                          np.eye(NH, dtype=f)], axis=0)
    nblf1 = np.where(idx[None, :] < idx[:, None], -f(BIG), f(0.0))
    nblr1 = np.where(idx[None, :] > idx[:, None], -f(BIG), f(0.0))
    nblf = np.tile(nblf1[:, None, :], (1, 4, 1)).reshape(128, 512)
    nblr = np.tile(nblr1[:, None, :], (1, 4, 1)).reshape(128, 512)
    ones1f = np.ones((1, 128), f)
    sel8 = np.zeros((NE, 8 * 128), f)
    for e_ in range(NE):
        sel8[e_, e_ * 128:(e_ + 1) * 128] = 1.0
    return dict(ulc=UL, urc=UR, sel127c=sel127, sel0c=sel0, idf32c=idf,
                idbfc=idf, ones2c=ones2, ohcc=ohc, nblfc=nblf.astype(f),
                nblrc=nblr.astype(f), ones1fc=ones1f, sel8c=sel8)


def _stage(inputs):
    gi = lambda n: np.asarray(inputs[n])
    x = gi("L0_attn_x")
    xT = np.ascontiguousarray(x.transpose(0, 2, 1))
    winT = np.ascontiguousarray(gi("L0_attn_W_in").T)
    woutT = np.ascontiguousarray(gi("L0_attn_W_out").T)
    consts = _host_consts()
    shared = dict(winT=winT, woutT=woutT,
                  router=np.ascontiguousarray(gi("L0_ffn_router_kernel_DE")),
                  gEDF=np.ascontiguousarray(gi("L0_ffn_kernel_gating_EDF")),
                  uEDF=np.ascontiguousarray(gi("L0_ffn_kernel_up_EDF")),
                  dEFD=np.ascontiguousarray(gi("L0_ffn_kernel_down_EFD")),
                  **consts)
    for d in ("f", "r"):
        shared[f"convw_{d}"] = np.ascontiguousarray(gi(f"L0_attn_conv_w_{d}"))
        shared[f"convb_{d}"] = np.ascontiguousarray(
            gi(f"L0_attn_conv_b_{d}")[:, None])
        shared[f"dtb_{d}"] = np.ascontiguousarray(gi(f"L0_attn_dt_bias_{d}")[:, None])
        shared[f"alog_{d}"] = np.ascontiguousarray(gi(f"L0_attn_A_log_{d}")[:, None])
        shared[f"drow_{d}"] = np.ascontiguousarray(gi(f"L0_attn_D_{d}")[None, :])
        shared[f"nwrow_{d}"] = np.ascontiguousarray(gi(f"L0_attn_norm_w_{d}")[None, :])
    in_maps = []
    for c in range(8):
        b, half = c // 2, c % 2
        fl = np.zeros(8, np.float32)
        if half == 0:
            fl[[0, 3, 4, 7]] = 1.0
        else:
            fl[[1, 2, 5, 6]] = 1.0
        m = dict(shared)
        m["xtok"] = np.ascontiguousarray(x[b, half * T:(half + 1) * T, :])
        m["xfm"] = np.ascontiguousarray(xT[b][:, half * T:(half + 1) * T])
        m["flagsc"] = np.tile(fl[None, :], (128, 1)).copy()
        in_maps.append(m)
    return in_maps


def run_cores(inputs):
    nc = get_program()
    in_maps = _stage(inputs)
    res = run_bass_kernel_spmd(nc, in_maps, core_ids=list(range(8)))
    return res


def kernel(**inputs):
    res = run_cores(inputs)
    out = np.empty((4, 4096, D), np.float32)
    for c in range(8):
        b, half = c // 2, c % 2
        ofm = np.asarray(res.results[c]["ofm"], dtype=np.float32).reshape(128, 8, T)
        out[b, half * T:(half + 1) * T, :] = (
            ofm.transpose(2, 1, 0).reshape(T, D))
    return out
